# version 07
# baseline (speedup 1.0000x reference)
import numpy as np
import ml_dtypes
from contextlib import ExitStack

import concourse.bass as bass
import concourse.mybir as mybir
from concourse.bass_utils import run_bass_kernel_spmd

BF = ml_dtypes.bfloat16
F32 = mybir.dt.float32
BF16 = mybir.dt.bfloat16
AF = mybir.ActivationFunctionType
OP = mybir.AluOpType
AX = mybir.AxisListType.X

B_, T_, D_ = 4, 8192, 1024
TH = 4096
CH = 128
NCH = TH // CH
TPT = 256
NT = TH // TPT
G = 8
H = 16
FF = 4096
FG = 32
EPS = 1e-5
LN30 = float(np.log(30.0))

EVEN_ODD = [0, 2, 4, 6, 8, 10, 12, 14, 1, 3, 5, 7, 9, 11, 13, 15]
DO_C = True
DBG = False
DO_SETUPB = True
DO_B = True


class Sched:
    def __init__(self):
        self.marks = {}
        self.counts = {}
        self.rec = True
        self.sems = {}

    def start(self, rec):
        self.rec = rec
        self.counts = {}
        self.trace = {}

    def bump(self, k, amt):
        self.counts[k] = self.counts.get(k, 0) + amt
        return self.counts[k]

    def mark(self, name, semkey):
        v = self.counts.get(semkey, 0)
        if self.rec:
            assert name not in self.marks, f"dup mark {name}"
            self.marks[name] = (semkey, v)
        else:
            assert self.marks[name] == (semkey, v), \
                f"mark drift {name}: {self.marks[name]} vs {(semkey, v)}"


class Eng:
    def __init__(self, sched, key, real):
        self.s = sched
        self.key = key
        self.real = real
        self.csem = {"tensor": "TE", "vector": "VE", "scalar": "SE",
                     "gpsimd": "GP"}.get(key)
        self.dsem = {"sync": "SYQ", "scalar": "SPQ", "gpsimd": "GPQ"}.get(key)
        self.hw = {}

    def op(self, method, *args, mark=None, **kw):
        doinc = mark is not None
        if self.s.rec and doinc:
            self.s.trace.setdefault(self.key, []).append(
                ("inc", self.csem, 1))
        if not self.s.rec:
            inst = getattr(self.real, method)(*args, **kw)
            if doinc:
                inst.then_inc(self.s.sems[self.csem], 1)
        if doinc:
            self.s.bump(self.csem, 1)
        if mark:
            self.s.mark(mark, self.csem)

    def dma(self, out, in_, mark=None, paced=True, **kw):
        if self.s.rec:
            self.s.trace.setdefault(self.key, []).append(
                ("inc", self.dsem, 16))
        self.s.bump(self.dsem, 16)
        if not self.s.rec:
            inst = self.real.dma_start(out=out, in_=in_, **kw)
            inst.then_inc(self.s.sems[self.dsem], 16)
            if paced:
                self.real.wait_ge(self.s.sems[self.dsem],
                                  self.s.counts[self.dsem])
        if mark:
            self.s.mark(mark, self.dsem)

    def fence(self):
        if not self.s.rec:
            self.real.wait_ge(self.s.sems[self.dsem],
                              self.s.counts[self.dsem])

    def mark_dma(self, name):
        self.s.mark(name, self.dsem)

    def wait(self, markname):
        if markname is None:
            return
        if self.s.rec:
            self.s.trace.setdefault(self.key, []).append(("waitm", markname))
            return
        semkey, v = self.s.marks[markname]
        if semkey == self.csem or v == 0:
            return
        if self.hw.get(semkey, 0) >= v:
            return
        self.real.wait_ge(self.s.sems[semkey], v)
        self.hw[semkey] = v

    def drain_(self):
        if not self.s.rec:
            self.real.drain()

    def wait_all_dma(self):
        v = self.s.counts.get(self.dsem, 0)
        if not self.s.rec:
            self.real.wait_ge(self.s.sems[self.dsem], v)


class Tens:
    def __init__(self, nc):
        self.nc = nc
        self.d = {}

    def add(self, stack, name, shape, dt):
        self.d[name] = stack.enter_context(
            self.nc.sbuf_tensor("sb_" + name, shape, dt))

    def addp(self, stack, name, shape, dt):
        self.d[name] = stack.enter_context(
            self.nc.psum_tensor("ps_" + name, shape, dt))

    def alias(self, name, other):
        self.d[name] = self.d[other]

    def __getattr__(self, n):
        if n in ("nc", "d"):
            raise AttributeError(n)
        return self.d[n]

    def __getitem__(self, n):
        return self.d[n]


def sync_A(e, t, p):
    for name in ("gamma8", "beta8", "mixk8", "mixv8", "mixr8", "mixw8",
                 "td8", "u8", "flag", "sel", "mask", "bpair", "ident",
                 "ones_f", "ones_b"):
        e.dma(t[name][:, :], p.dri[name])
    e.dma(t.xprevrow[0:1, :], p.dri["xprev"])
    for w in ("wk", "wv", "wr", "ww", "wg"):
        e.dma(t[w][:, :].rearrange("q (a d) -> q a d", a=G),
              p.dri[w].rearrange("(a q) d -> q a d", q=128),
              mark=f"ld:{w}")
    e.s.mark("ld:setup", "SYQ")
    for ti in range(NT):
        for cc in (2 * ti, 2 * ti + 1):
            if cc >= 2:
                e.wait(f"xdone:{cc - 2}")
            e.dma(t.xin[cc % 2][:, :],
                  p.dri["x"][cc * CH:(cc + 1) * CH, :], mark=f"x:{cc}")
        if ti == 1 and DBG in (True, "a"):
            for nm, mk in (("ktil", "kev:0:3"), ("rhat", "rhat:0"),
                           ("vtm", "vev:1"), ("lin", "scan:1"),
                           ("mxk", "mix:k:0"), ("mxw", "mix:w:0"),
                           ("w1", "w1:0"), ("fk", "fk:0"), ("fr", "fr:0"),
                           ("msc", "mev:1:3"), ("xsb", "xsbc:1"),
                           ("cst", "cout:0"), ("ktm", "ktmv:0")):
                e.wait(mk)
                e.dma(p.dbg[nm], t[nm][:, :])
            e.wait("supd:1")
            e.dma(p.dbg["S2"], t.S[:, :])
        if ti >= 1:
            spills(e, t, p, ti - 1)
    spills(e, t, p, NT - 1)
    if DO_B and DO_C:
        e.wait(f"supd:{NCH - 1}")
        e.dma(p.cc_in[:, :], t.S[:, :], mark="cc:S")


def spills(e, t, p, ti):
    for ci, cc in enumerate((2 * ti, 2 * ti + 1)):
        e.wait(f"yev:{cc}")
        e.dma(p.y_sp[cc, :, :], t.ysp[:, ci * 1024:(ci + 1) * 1024],
              mark=f"spy:{cc}")
        e.wait(f"g:{cc}")
        e.dma(p.g_sp[cc, :, :], t.gsp[:, ci * 1024:(ci + 1) * 1024],
              mark=f"spg:{cc}")
    e.wait(f"rhat:{ti}")
    for ci, cc in enumerate((2 * ti, 2 * ti + 1)):
        e.dma(p.r_sp[cc, :, :],
              bass.AP(t.rhat, ci * 128, [[2048, 128], [256, G], [1, 128]]),
              mark=f"spr:{cc}")


def tensor_A(e, t, p):
    ID = t.ident[:, :]
    g_ = {}

    def use(ps, freemark):
        e.wait(g_.get(ps))
        g_[ps] = freemark

    e.wait("ld:setup")
    for w, mk in ((("wk", "bk"), ("wr", "br"), ("ww", "bw")) if DO_SETUPB
                  else ()):
        e.wait(f"ld:{w}")
        e.wait("su:betab")
        use("ppa", f"su:{mk}out")
        for dg in range(G):
            for gi in range(G):
                e.op("matmul", t.ppa[:, dg:dg + 1],
                     t[w][:, gi * 1024 + dg * 128:gi * 1024 + (dg + 1) * 128],
                     t.beta_b[:, gi:gi + 1], start=(gi == 0), stop=(gi == 7),
                     mark=(f"su:{mk}" if (dg == G - 1 and gi == 7) else None))
    for w, mk in ((("wv", "bv"), ("wg", "bg")) if DO_SETUPB else ()):
        e.wait(f"ld:{w}")
        for half in range(2):
            use("ppb", f"su:{mk}rowout{half}")
            for gi in range(G):
                e.op("matmul", t.ppb[0:1, 0:512], t.beta_b[:, gi:gi + 1],
                     t[w][:, gi * 1024 + half * 512:gi * 1024 + half * 512 + 512],
                     start=(gi == 0), stop=(gi == 7),
                     mark=(f"su:{mk}row{half}" if gi == 7 else None))
            e.wait(f"su:{mk}rowout{half}")
        use("ppv", f"su:{mk}full")
        for half in range(2):
            e.op("matmul", t.ppv[:, half * 512:(half + 1) * 512],
                 t.ones_b[0:1, :],
                 t[mk + "row"][0:1, half * 512:(half + 1) * 512],
                 start=True, stop=True,
                 mark=(f"su:{mk}bc" if half == 1 else None))
    if DO_SETUPB:
        e.wait("su:xprevpad")
        use("pst", "su:prevcol0")
        for g in range(G):
            e.op("transpose", t.pst[:, g * 128:(g + 1) * 128],
                 t.xprevpad[:, g * 128:(g + 1) * 128], ID,
                 mark=("su:pxt" if g == G - 1 else None))
    e.wait("su:fold")

    for ti in range(NT):
        ccs = (2 * ti, 2 * ti + 1)
        for ci, cc in enumerate(ccs):
            e.wait(f"xh:{cc}")
            use("pst", f"xsbc:{cc}")
            for g in range(G):
                e.op("transpose", t.pst[:, g * 128:(g + 1) * 128],
                     t.xh2[:, g * 128:(g + 1) * 128], ID,
                     mark=(f"xt:{cc}" if g == G - 1 else None))
        for w, nm in (("ww", "w"), ("wk", "k"), ("wr", "r")):
            e.wait(f"mix:{nm}:{ti}")
            for dp in range(4):
                ps = "ppa" if dp % 2 == 0 else "ppb"
                free = {"w": f"wev:{ti}:{dp}", "k": f"kevp:{ti}:{dp}",
                        "r": f"rsig:{ti}:{dp}"}[nm]
                use(ps, free)
                for sub in range(2):
                    dg = dp * 2 + sub
                    for gi in range(G):
                        e.op("matmul",
                             t[ps][:, sub * 256:(sub + 1) * 256],
                             t[w][:, gi * 1024 + dg * 128:gi * 1024 + (dg + 1) * 128],
                             t["mx" + nm][:, gi * 256:(gi + 1) * 256],
                             start=(gi == 0), stop=(gi == 7),
                             mark=(f"{nm}p:{ti}:{dp}" if (sub == 1 and gi == 7)
                                   else None))
        for ci, cc in enumerate(ccs):
            for w, nm, mxn in (("wv", "v", "mxv"), ("wg", "g", "mxr")):
                e.wait(f"mix:{'v' if nm == 'v' else 'r'}:{ti}")
                use("ppv", f"{nm}ev:{cc}")
                for gi in range(G):
                    for half in range(2):
                        e.op("matmul", t.ppv[:, half * 512:(half + 1) * 512],
                             t[mxn][:, gi * 256 + ci * 128:gi * 256 + ci * 128 + 128],
                             t[w][:, gi * 1024 + half * 512:gi * 1024 + half * 512 + 512],
                             start=(gi == 0), stop=(gi == 7),
                             mark=(f"{nm}p:{cc}" if (gi == 7 and half == 1)
                                   else None))
        for ci, cc in enumerate(ccs):
            if not DO_B:
                continue
            e.wait(f"kev:{ti}:3")
            use("pst", f"ktmv:{cc}")
            for g in range(G):
                e.op("transpose", t.pst[:, g * 128:(g + 1) * 128],
                     t.ktil[:, g * 256 + ci * 128:g * 256 + ci * 128 + 128], ID,
                     mark=(f"ktm:{cc}" if g == G - 1 else None))
            e.wait(f"rhat:{ti}")
            for wv in range(4):
                hs = [(wv // 2) * 8 + (wv % 2) + 2 * i for i in range(4)]
                ps = "pca" if wv % 2 == 0 else "pcb"
                use(ps, f"mev:{cc}:{wv}")
                for i, h in enumerate(hs):
                    g = h // 2
                    ksrc = t.ktil_e if h % 2 == 0 else t.ktil_o
                    e.op("matmul", t[ps][:, i * 128:(i + 1) * 128],
                         ksrc[:, g * 256 + ci * 128:g * 256 + ci * 128 + 128],
                         t.rhat[:, g * 256 + ci * 128:g * 256 + ci * 128 + 128],
                         start=True, stop=True,
                         mark=(f"pw:{cc}:{wv}" if i == 3 else None))
            e.wait(f"mev:{cc}:3")
            e.wait(f"vev:{cc}")
            e.wait(f"gev:{cc}")
            use("ppv", f"yev:{cc}")
            if cc > 0:
                e.wait(f"sbf:{cc - 1}")
            for bank in range(2):
                for h in range(bank * 8, bank * 8 + 8):
                    e.op("matmul", t.ppv[:, h * 64:(h + 1) * 64],
                         t.msc[:, h * 128:(h + 1) * 128],
                         t.vtm[:, ci * 1024 + h * 64:ci * 1024 + (h + 1) * 64],
                         start=(h % 8 == 0), stop=False)
                for h in range(bank * 8, bank * 8 + 8):
                    g = h // 2
                    e.op("matmul", t.ppv[:, h * 64:(h + 1) * 64],
                         t.rhat[:, g * 256 + ci * 128:g * 256 + ci * 128 + 128],
                         t.sbf[:, (h % 2) * 512 + g * 64:(h % 2) * 512 + (g + 1) * 64],
                         start=False, stop=(h % 8 == 7),
                         mark=(f"ycross:{cc}" if h == 15 else None))
            e.wait(f"ktmv:{cc}")
            use("pu", f"supd:{cc}")
            for g in range(G):
                for sub in range(2):
                    h = 2 * g + sub
                    e.op("matmul",
                         t.pu[sub * 64:(sub + 1) * 64, g * 64:(g + 1) * 64],
                         t.ktm[:, h * 64:(h + 1) * 64],
                         t.vtm[:, ci * 1024 + h * 64:ci * 1024 + (h + 1) * 64],
                         start=True, stop=True, tile_position=(0, sub * 64),
                         mark=(f"uu:{cc}" if (g == G - 1 and sub == 1) else None))
            e.wait(f"ruk:{cc}")
            use("ppa", f"cout:{cc}")
            for g in range(G):
                e.op("matmul", t.ppa[:, 2 * g:2 * g + 2],
                     t.ruk[:, g * 128:(g + 1) * 128], t.bpair[:, :],
                     start=True, stop=True,
                     mark=(f"c:{cc}" if g == G - 1 else None))
    e.s.mark("endA:tensor", "TE")


def vector_A(e, t, p):
    e.wait("ld:setup")
    e.op("memset", t.epsc[:, :], EPS, mark="su:eps")
    e.op("memset", t.S[:, :], 0.0)
    e.op("memset", t.sbf[:, :], 0.0)
    e.op("memset", t.ktil_e[:, :], 0.0)
    e.op("memset", t.ktil_o[:, :], 0.0)
    e.op("memset", t.gpre[:, :], 0.0)
    e.op("memset", t.lin[:, :], 0.0)
    e.op("memset", t.xprevpad[:, :], 0.0)
    e.op("tensor_copy", t.beta_b[:, :], t.beta8[:, :], mark="su:betab")
    if DO_SETUPB:
        for mk, dst in (("bk", "bk8"), ("br", "br8"), ("bw", "bw8")):
            e.wait(f"su:{mk}")
            e.op("tensor_copy", t[dst][:, :], t.ppa[:, 0:8], mark=f"su:{mk}out")
        for mk in ("bv", "bg"):
            for half in range(2):
                e.wait(f"su:{mk}row{half}")
                e.op("tensor_copy",
                     t[mk + "row"][0:1, half * 512:(half + 1) * 512],
                     t.ppb[0:1, 0:512], mark=f"su:{mk}rowout{half}")
            e.wait(f"su:{mk}bc")
            e.op("tensor_copy", t[mk + "full"][:, :], t.ppv[:, :],
                 mark=f"su:{mk}full")
    else:
        for dst in ("bk8", "br8", "bw8"):
            e.op("memset", t[dst][:, :], 0.0, mark=f"su:{dst[:2]}out")
        e.op("memset", t.bvfull[:, :], 0.0, mark="su:bvfull")
        e.op("memset", t.bgfull[:, :], 0.0, mark="su:bgfull")
    e.wait("su:bwout")
    e.drain_()
    e.op("tensor_tensor", t.tdb8[:, :], t.bw8[:, :], t.td8[:, :], OP.add,
         mark="su:tdb")
    for w in ("wk", "wv", "wr", "ww", "wg"):
        for gi in range(G):
            e.op("tensor_scalar", t[w][:, gi * 1024:(gi + 1) * 1024],
                 t[w][:, gi * 1024:(gi + 1) * 1024],
                 t.gamma8[:, gi:gi + 1], None, OP.mult,
                 mark=("su:fold" if (w == "wg" and gi == G - 1) else None))
    xp = t.xprevrow
    for it in range(2):
        e.op("reduce_sum", t.tiny[0:1, 0:1], xp[0:1, :], AX)
        e.op("scalar_tensor_tensor", t.tinyb[0:1, :], xp[0:1, :], 1.0,
             xp[0:1, :], OP.bypass, OP.mult, accum_out=t.tiny[0:1, 1:2])
        e.drain_()
        e.op("tensor_scalar", t.tiny[0:1, 0:2], t.tiny[0:1, 0:2],
             1.0 / 1024.0, None, OP.mult)
        e.drain_()
        e.op("tensor_tensor", t.tiny[0:1, 2:3], t.tiny[0:1, 0:1],
             t.tiny[0:1, 0:1], OP.mult)
        e.drain_()
        e.op("tensor_tensor", t.tiny[0:1, 2:3], t.tiny[0:1, 1:2],
             t.tiny[0:1, 2:3], OP.subtract)
        e.drain_()
        e.op("tensor_scalar", t.tiny[0:1, 2:3], t.tiny[0:1, 2:3], EPS, None,
             OP.add, mark=f"su:xpv{it}")
        e.wait(f"su:xps{it}")
        e.drain_()
        e.op("tensor_scalar", xp[0:1, :], xp[0:1, :],
             t.tiny[0:1, 0:1], t.tiny[0:1, 4:5], OP.subtract, OP.mult)
        e.drain_()
    e.op("tensor_copy", t.xprevpad[0:1, :], xp[0:1, :], mark="su:xprevpad")
    if DO_SETUPB:
        e.wait("su:pxt")
        for g in range(G):
            e.op("tensor_copy", t.prevcol[:, g:g + 1],
                 t.pst[:, g * 128:g * 128 + 1])
        e.drain_()
        e.op("tensor_scalar", t.prevcol[:, :], t.prevcol[:, :],
             t.flag[:, 0:1], None, OP.mult, mark="su:prevcol0")
    else:
        e.op("memset", t.prevcol[:, :], 0.0, mark="su:prevcol0")

    for ti in range(NT):
        ccs = (2 * ti, 2 * ti + 1)
        for ci, cc in enumerate(ccs):
            for ln in (1, 2):
                st = t["st%d" % ln]
                e.wait(f"l{ln}acc:{cc}")
                e.op("scalar_tensor_tensor", st[:, 2:3], st[:, 0:1],
                     1.0 / (1024.0 * 1024.0), st[:, 0:1], OP.mult, OP.mult)
                e.drain_()
                e.op("scalar_tensor_tensor", st[:, 5:6], st[:, 1:2],
                     1.0 / 1024.0, st[:, 2:3], OP.mult, OP.subtract,
                     mark=f"l{ln}var:{cc}")
                e.wait(f"l{ln}s:{cc}")
                e.op("scalar_tensor_tensor", st[:, 4:5], st[:, 0:1],
                     -1.0 / 1024.0, st[:, 3:4], OP.mult, OP.mult,
                     mark=f"l{ln}b:{cc}")
        if ti == 0:
            e.wait("su:prevcol0")
            e.op("tensor_copy",
                 bass.AP(t.xsb, 0, [[G * 257, 128], [257, G], [1, 1]]),
                 t.prevcol[:, :].unsqueeze(2))
        else:
            e.wait(f"gmix:w:{ti - 1}")
            e.op("tensor_copy",
                 bass.AP(t.xsb, 0, [[G * 257, 128], [257, G], [1, 1]]),
                 bass.AP(t.xsb, 256, [[G * 257, 128], [257, G], [1, 1]]))
        for ci, cc in enumerate(ccs):
            e.wait(f"xt:{cc}")
            e.op("tensor_copy",
                 bass.AP(t.xsb, 1 + ci * 128,
                         [[G * 257, 128], [257, G], [1, 128]]),
                 t.pst[:, :].rearrange("q (a b) -> q a b", a=G),
                 mark=f"xsbc:{cc}")
        xcur = bass.AP(t.xsb, 1, [[G * 257, 128], [257, G], [1, 256]])
        xprv = bass.AP(t.xsb, 0, [[G * 257, 128], [257, G], [1, 256]])
        e.drain_()
        e.op("tensor_tensor", t.tmx[:, :].rearrange("q (a b) -> q a b", a=G),
             xcur, xprv, OP.subtract, mark=f"tmix:{ti}")
        for nm in ("k", "v", "r", "w"):
            e.wait(f"gmix:{nm}:{ti}")
            e.op("tensor_tensor",
                 t["mx" + nm][:, :].rearrange("q (a b) -> q a b", a=G),
                 t["mx" + nm][:, :].rearrange("q (a b) -> q a b", a=G),
                 xprv, OP.add, mark=f"mix:{nm}:{ti}")
        if ti > 0:
            e.wait(f"expw:{2 * ti - 1}")
        for dp in range(4):
            ps = t.ppa if dp % 2 == 0 else t.ppb
            e.wait(f"wp:{ti}:{dp}")
            for sub in range(2):
                dg = dp * 2 + sub
                e.op("tensor_scalar", t.w1[:, dg * 256:(dg + 1) * 256],
                     ps[:, sub * 256:(sub + 1) * 256],
                     t.tdb8[:, dg:dg + 1], LN30, OP.add, OP.min,
                     mark=(f"wev:{ti}:{dp}" if sub == 1 else None))
        e.drain_()
        e.op("tensor_single_scalar", t.w1[:, :], t.w1[:, :], -10.0, OP.max,
             mark=f"w1:{ti}")
        for ci, cc in enumerate(ccs):
            e.wait(f"expw:{cc}")
            if ti > 0 and ci == 0:
                e.wait(f"fk:{ti - 1}")
                e.wait(f"elc:{2 * ti - 1}")
                e.wait(f"eg:{2 * ti - 1}")
            for g in range(G):
                e.op("tensor_tensor_scan",
                     t.lin[:, g * 258 + ci * 129 + 1:g * 258 + ci * 129 + 129],
                     t.ones_f[:, 0:128], t.expw[:, g * 128:(g + 1) * 128],
                     0.0, OP.mult, OP.subtract,
                     mark=(f"scan:{cc}" if g == G - 1 else None))
        for dp in range(4):
            ps = t.ppa if dp % 2 == 0 else t.ppb
            e.wait(f"kp:{ti}:{dp}")
            e.wait(f"fk:{ti}")
            if ti > 0 and dp == 0 and DO_B:
                e.wait(f"pw:{2 * ti - 1}:3")
                e.wait(f"ktm:{2 * ti - 1}")
                e.wait(f"rk:{2 * ti - 1}")
            for sub in range(2):
                dg = dp * 2 + sub
                e.op("scalar_tensor_tensor",
                     t.ktil[:, dg * 256:(dg + 1) * 256],
                     ps[:, sub * 256:(sub + 1) * 256], t.bk8[:, dg:dg + 1],
                     t.fk[:, dg * 256:(dg + 1) * 256], OP.add, OP.mult,
                     mark=(f"kevp:{ti}:{dp}" if sub == 1 else None))
        e.drain_()
        for dp in range(4):
            for sub in range(2):
                dg = dp * 2 + sub
                e.op("tensor_copy", t.ktil_e[0:64, dg * 256:(dg + 1) * 256],
                     t.ktil[0:64, dg * 256:(dg + 1) * 256])
                e.op("tensor_copy", t.ktil_o[64:128, dg * 256:(dg + 1) * 256],
                     t.ktil[64:128, dg * 256:(dg + 1) * 256],
                     mark=(f"kev:{ti}:{dp}" if sub == 1 else None))
        for dp in range(4):
            e.wait(f"rsig:{ti}:{dp}")
            e.wait(f"fr:{ti}")
            if ti > 0 and dp == 0:
                if DO_B:
                    e.wait(f"ycross:{2 * ti - 1}")
                e.wait(f"spr:{2 * ti - 1}")
            e.op("tensor_tensor", t.rhat[:, dp * 512:(dp + 1) * 512],
                 t.rsg[:, :], t.fr[:, dp * 512:(dp + 1) * 512], OP.mult,
                 mark=(f"rhat:{ti}" if dp == 3 else f"rhev:{ti}:{dp}"))
        for ci, cc in enumerate(ccs):
            e.wait(f"vp:{cc}")
            if cc > 1 and DO_B:
                e.wait(f"uu:{cc - 2}")
            e.op("scalar_tensor_tensor", t.vtm[:, ci * 1024:(ci + 1) * 1024],
                 t.ppv[:, :], 1.0, t.bvfull[:, :], OP.bypass, OP.add,
                 mark=f"vev:{cc}")
            e.wait(f"gp:{cc}")
            if cc > 0:
                e.wait(f"g:{cc - 1}")
            e.op("scalar_tensor_tensor", t.gst[:, :], t.ppv[:, :], 1.0,
                 t.bgfull[:, :], OP.bypass, OP.add, mark=f"gev:{cc}")
        for ci, cc in enumerate(ccs):
            if not DO_B:
                e.op("memset", t.ysp[:, ci * 1024:(ci + 1) * 1024], 0.0,
                     mark=f"yev:{cc}")
                continue
            e.wait(f"ktm:{cc}")
            if cc > 0:
                e.wait(f"uu:{cc - 1}")
            e.op("tensor_copy", t.ktm[:, :],
                 t.pst[:, :].rearrange("q (a b) -> q a b", a=G),
                 mark=f"ktmv:{cc}")
            for wv in range(4):
                ps = t.pca if wv % 2 == 0 else t.pcb
                hs = [(wv // 2) * 8 + (wv % 2) + 2 * i for i in range(4)]
                e.wait(f"pw:{cc}:{wv}")
                if cc > 0 and wv == 0:
                    e.wait(f"ycross:{cc - 1}")
                e.op("tensor_tensor",
                     bass.AP(t.msc, hs[0] * 128,
                             [[2048, 128], [256, 4], [1, 128]]),
                     ps[:, :].rearrange("q (a b) -> q a b", a=4),
                     t.mask[:, :].rearrange("q (a b) -> q a b", a=4),
                     OP.mult, mark=f"mev:{cc}:{wv}")
            e.drain_()
            e.op("tensor_tensor", t.ruk[:, :].rearrange("q (a b) -> q a b", a=G),
                 bass.AP(t.rhat, ci * 128, [[2048, 128], [256, G], [1, 128]]),
                 bass.AP(t.ktil, ci * 128, [[2048, 128], [256, G], [1, 128]]),
                 OP.mult, mark=f"rk:{cc}")
            e.wait(f"uu:{cc}")
            e.wait(f"ycross:{cc}")
            e.op("tensor_tensor", t.S[:, :], t.S[:, :], t.pu[:, :], OP.add)
            e.wait(f"elc:{cc}")
            e.drain_()
            for g in range(G):
                e.op("tensor_scalar", t.S[:, g * 64:(g + 1) * 64],
                     t.S[:, g * 64:(g + 1) * 64], t.elc[:, g:g + 1], None,
                     OP.mult, mark=(f"supd:{cc}" if g == G - 1 else None))
            e.drain_()
            e.op("tensor_copy", t.sbf[0:64, 0:512], t.S[0:64, :])
            e.op("tensor_copy", t.sbf[64:128, 512:1024], t.S[64:128, :],
                 mark=f"sbf:{cc}")
            e.wait(f"eg:{cc}")
            e.op("tensor_tensor", t.gpre[:, :], t.gpre[:, :],
                 bass.AP(t.lin, ci * 129 + 128, [[G * 258, 128], [258, G]]),
                 OP.add, mark=f"gpreu:{cc}")
            e.wait(f"c:{cc}")
            e.op("tensor_copy", t.cst[:, :], t.ppa[:, 0:16], mark=f"cout:{cc}")
            e.drain_()
            e.op("tensor_tensor", t.vc[:, :].rearrange("q (a b) -> q a b", a=H),
                 t.vtm[:, ci * 1024:(ci + 1) * 1024]
                 .rearrange("q (a b) -> q a b", a=H),
                 t.cst[:, :].unsqueeze(2).broadcast_to([128, H, 64]), OP.mult)
            if cc > 1:
                e.wait(f"spy:{cc - 2}")
            e.drain_()
            e.op("tensor_tensor", t.ysp[:, ci * 1024:(ci + 1) * 1024],
                 t.ppv[:, :], t.vc[:, :], OP.add, mark=f"yev:{cc}")
    e.s.mark("endA:vector", "VE")


def scalar_A(e, t, p):
    e.wait("ld:setup")
    e.wait("su:eps")
    for nm in ("k", "v", "r", "w"):
        e.op("activation", t["sg" + nm][:, :], t["mix" + nm + "8"][:, :],
             AF.Sigmoid, mark=f"su:sg{nm}")
    for it in range(2):
        e.wait(f"su:xpv{it}")
        e.op("activation", t.tiny[0:1, 4:5], t.tiny[0:1, 2:3], AF.Ln)
        e.op("activation", t.tiny[0:1, 4:5], t.tiny[0:1, 4:5], AF.Exp,
             scale=-0.5, mark=f"su:xps{it}")
    for ti in range(NT):
        ccs = (2 * ti, 2 * ti + 1)
        for ci, cc in enumerate(ccs):
            e.wait(f"x:{cc}")
            xv = t.xin[cc % 2][:, :]
            e.op("activation", t.dump[:, :], xv, AF.Copy,
                 accum_out=t.st1[:, 0:1])
            e.op("activation", t.dump[:, :], xv, AF.Square,
                 accum_out=t.st1[:, 1:2], mark=f"l1acc:{cc}")
            e.wait(f"l1var:{cc}")
            e.op("activation", t.st1[:, 3:4], t.st1[:, 5:6], AF.Ln, bias=t.epsc[:, 0:1])
            e.op("activation", t.st1[:, 3:4], t.st1[:, 3:4], AF.Exp,
                 scale=-0.5, mark=f"l1s:{cc}")
            e.wait(f"l1b:{cc}")
            e.op("activation", t.xh1[:, :], xv, AF.Identity,
                 scale=t.st1[:, 3:4], bias=t.st1[:, 4:5],
                 accum_out=t.st2[:, 0:1], mark=f"xdone:{cc}")
            e.op("activation", t.dump[:, :], t.xh1[:, :], AF.Square,
                 accum_out=t.st2[:, 1:2], mark=f"l2acc:{cc}")
            e.wait(f"l2var:{cc}")
            e.op("activation", t.st2[:, 3:4], t.st2[:, 5:6], AF.Ln, bias=t.epsc[:, 0:1])
            e.op("activation", t.st2[:, 3:4], t.st2[:, 3:4], AF.Exp,
                 scale=-0.5, mark=f"l2s:{cc}")
            e.wait(f"l2b:{cc}")
            if cc > 0:
                e.wait(f"xt:{cc - 1}")
            e.op("activation", t.xh2[:, :], t.xh1[:, :], AF.Identity,
                 scale=t.st2[:, 3:4], bias=t.st2[:, 4:5], mark=f"xh:{cc}")
            if DBG in (True, "a") and cc == 0:
                e.dma(p.dbg["st1"], t.st1[:, :])
                e.dma(p.dbg["st2"], t.st2[:, :])
                e.dma(p.dbg["xh1"], t.xh1[:, :])
                e.dma(p.dbg["xh2"], t.xh2[:, :])
                e.dma(p.dbg["xin"], t.xin[0][:, :])
                e.wait_all_dma()
        for ci, cc in enumerate(ccs):
            e.wait(f"w1:{ti}")
            if cc > 0:
                e.wait(f"scan:{cc - 1}")
            e.op("activation", t.expw[:, :],
                 bass.AP(t.w1, ci * 128, [[2048, 128], [256, G], [1, 128]]),
                 AF.Exp, mark=f"expw:{cc}")
        e.wait(f"scan:{2 * ti + 1}")
        if ti > 0:
            e.wait(f"kev:{ti - 1}:3")
            e.wait(f"rhat:{ti - 1}")
        e.op("activation", t.fr[:, :].rearrange("q (a b c) -> q a b c",
                                                a=G, b=2),
             bass.AP(t.lin, 0, [[G * 258, 128], [258, G], [129, 2], [1, 128]]),
             AF.Exp, mark=f"fr:{ti}")
        e.op("activation", t.fk[:, :].rearrange("q (a b c) -> q a b c",
                                                a=G, b=2),
             bass.AP(t.lin, 1, [[G * 258, 128], [258, G], [129, 2], [1, 128]]),
             AF.Exp, scale=-1.0, mark=f"fk:{ti}")
        for dp in range(4):
            ps = t.ppa if dp % 2 == 0 else t.ppb
            e.wait(f"rp:{ti}:{dp}")
            if dp > 0:
                e.wait(f"rhev:{ti}:{dp - 1}")
            for sub in range(2):
                dg = dp * 2 + sub
                e.op("activation", t.rsg[:, sub * 256:(sub + 1) * 256],
                     ps[:, sub * 256:(sub + 1) * 256], AF.Sigmoid,
                     bias=t.br8[:, dg:dg + 1],
                     mark=(f"rsig:{ti}:{dp}" if sub == 1 else None))
        for ci, cc in enumerate(ccs):
            e.wait(f"gev:{cc}")
            if cc > 1:
                e.wait(f"spg:{cc - 2}")
            e.op("activation", t.gsp[:, ci * 1024:(ci + 1) * 1024],
                 t.gst[:, :], AF.Sigmoid, mark=f"g:{cc}")
        for ci, cc in enumerate(ccs):
            if DO_B:
                e.wait(f"plg:{cc}")
                e.op("activation", t.dcy[:, :], t.dcy[:, :], AF.Exp,
                     mark=f"d:{cc}")
        for ci, cc in enumerate(ccs):
            if not DO_B:
                continue
            if cc > 0:
                e.wait(f"supd:{cc - 1}")
            e.op("activation", t.elc[:, :],
                 bass.AP(t.lin, ci * 129 + 128, [[G * 258, 128], [258, G]]),
                 AF.Exp, mark=f"elc:{cc}")
            e.wait(f"gpreu:{cc - 1}" if cc > 0 else "su:betab")
            e.op("activation", t.eg[:, cc * 8:(cc + 1) * 8], t.gpre[:, :],
                 AF.Exp, mark=f"eg:{cc}")
    e.s.mark("endA:scalar", "SE")


def gpsimd_A(e, t, p):
    for ti in range(NT):
        e.wait(f"tmix:{ti}")
        prev_reader = {"k": f"kp:{ti - 1}:3", "v": f"vp:{2 * ti - 1}",
                       "r": f"gp:{2 * ti - 1}", "w": f"wp:{ti - 1}:3"}
        for nm in ("k", "v", "r", "w"):
            e.wait(f"su:sg{nm}")
            if ti > 0:
                e.wait(prev_reader[nm])
                if nm == "r":
                    e.wait(f"rp:{ti - 1}:3")
            e.op("tensor_tensor",
                 t["mx" + nm][:, :].rearrange("q (a b) -> q a b", a=G),
                 t.tmx[:, :].rearrange("q (a b) -> q a b", a=G),
                 t["sg" + nm][:, :].unsqueeze(2).broadcast_to([128, G, 256]),
                 OP.mult, mark=f"gmix:{nm}:{ti}")
        for ci, cc in enumerate((2 * ti, 2 * ti + 1)):
            if not DO_B:
                continue
            e.wait(f"scan:{cc}")
            e.op("tensor_tensor", t.dcy[:, :].rearrange("q (a b) -> q a b", a=G),
                 bass.AP(t.lin, ci * 129 + 1, [[G * 258, 128], [258, G], [1, 128]]),
                 bass.AP(t.lin, ci * 129, [[G * 258, 128], [258, G], [1, 128]]),
                 OP.subtract, mark=f"plg:{cc}")
            e.wait(f"d:{cc}")
            e.op("tensor_tensor", t.dcy[:, :].rearrange("q (a b) -> q a b", a=G),
                 t.dcy[:, :].rearrange("q (a b) -> q a b", a=G),
                 t.u8[:, :].unsqueeze(2).broadcast_to([128, G, 128]),
                 OP.mult, mark=f"ud:{cc}")
            e.wait(f"rk:{cc}")
            if cc > 0:
                e.wait(f"c:{cc - 1}")
            e.op("tensor_tensor", t.ruk[:, :], t.ruk[:, :], t.dcy[:, :],
                 OP.mult, mark=f"ruk:{cc}")
    if DO_B and DO_C:
        e.wait("cc:S")
        e.op("collective_compute", "AllGather", OP.bypass,
             replica_groups=[list(range(8))], ins=[p.cc_in[:, :]],
             outs=[p.cc_out[:, :, :]], mark="cc:done")
    e.s.mark("endA:gpsimd", "GP")


def sync_C(e, t, p):
    for k in ("tensor", "vector", "scalar", "gpsimd"):
        e.wait(f"endA:{k}")
    e.dma(t.wo[:, :].rearrange("q (a d) -> q a d", a=G),
          p.dri["wo"].rearrange("(a q) d -> q a d", q=128), mark="ld:wo")
    for i in range(4):
        e.dma(t.wfc[:, i * 8192:(i + 1) * 8192]
              .rearrange("q (a d) -> q a d", a=2),
              p.dri["wfc"][i * 256:(i + 1) * 256, :]
              .rearrange("(a q) d -> q a d", q=128),
              mark=f"ld:wfc{i}")
    e.wait("cc:done")
    e.dma(t.sall[:, :].rearrange("q (r d) -> q r d", r=8),
          p.cc_out.rearrange("r q d -> q r d"), mark="ld:sall")
    for i in range(4):
        e.dma(t.wpj[:, i * 8192:(i + 1) * 8192]
              .rearrange("q (a d) -> q a d", a=8),
              p.dri["wpj"][i * 1024:(i + 1) * 1024, :]
              .rearrange("(a q) d -> q a d", q=128),
              mark=f"ld:wpj{i}")
    if DBG in (True, "c", "c2"):
        e.wait("ssel")
        e.dma(p.dbg["ssl"], t.ssl[:, :])
        e.dma(p.dbg["sallo"], t.sall[:, :])
        e.dma(p.dbg["eg"], t.eg[:, :])
    for cc in range(NCH):
        if DBG == "z1" and cc == 1:
            e.wait("rg:0")
        if DBG == "z1" and cc == 2:
            e.wait("x1:0")
            e.wait("ytv:0")
        if DBG == "z5" and cc == 1:
            e.wait("x1:0")
        if DBG in (True, "c", "c3") and cc == 1:
            e.wait("rg:0")
            e.dma(p.dbg["rg0"], t.rg[:, :])
            e.wait("dbg:rg0")
            e.dma(p.dbg["rgv"], t.dbgrg[:, :])
            e.wait("dbg:yt0")
            e.dma(p.dbg["ytv0"], t.dbgyt[:, :])
        if DBG in (True, "c", "c3") and cc == 2:
            e.wait("x1:0")
            e.dma(p.dbg["x10"], t.x1c[0][:, :])
            e.wait("ytv:0")
            e.dma(p.dbg["yt0"], t.yt[:, :])
        if cc >= 2:
            e.wait(f"out:{cc - 2}")
        e.dma(t.xc[cc % 2][:, :], p.dri["x"][cc * CH:(cc + 1) * CH, :])
        if cc >= 2:
            e.wait(f"ytv:{cc - 2}")
        e.dma(t.yre[cc % 2][:, :], p.y_sp[cc, :, :])
        if cc >= 2:
            e.wait(f"gy:{cc - 2}")
        e.dma(t.gre[cc % 2][:, :], p.g_sp[cc, :, :])
        if cc >= 2:
            e.wait(f"rg:{cc - 2}")
        e.dma(t.rre[cc % 2][:, :], p.r_sp[cc, :, :])
        e.fence()
        for nm in ("cx", "cy", "cg", "cr"):
            e.mark_dma(f"{nm}:{cc}")
        if cc >= 1:
            e.wait(f"outv:{cc - 1}")
            e.dma(p.dro[(cc - 1) * CH:cc * CH, :],
                  t.outst[(cc - 1) % 2][:, :])
            e.fence()
            e.mark_dma(f"out:{cc - 1}")
    e.wait(f"outv:{NCH - 1}")
    e.dma(p.dro[(NCH - 1) * CH:NCH * CH, :], t.outst[(NCH - 1) % 2][:, :],
          mark=f"out:{NCH - 1}")
    if DBG in (True, "a"):
        e.dma(p.dbg["ysp"], p.y_sp[:, :, :])
        e.dma(p.dbg["gsp"], p.g_sp[:, :, :])
        e.dma(p.dbg["rsp"], p.r_sp[:, :, :])
        e.dma(p.dbg["Send"], p.cc_in[:, :])
        e.wait(f"x1:{NCH - 1}")
        e.dma(p.dbg["x1"], t.x1c[1][:, :])
        e.wait(f"hsq:{NT - 1}:31")
        e.dma(p.dbg["hb"], t.hb[:, :])
    e.wait_all_dma()


def tensor_C(e, t, p):
    for k in ("vector", "scalar", "gpsimd"):
        e.wait(f"endA:{k}")
    g_ = {}

    def use(ps, freemark):
        e.wait(g_.get(ps))
        g_[ps] = freemark

    for cc in range(NCH):
        ci = cc % 2
        ti = cc // 2
        e.wait(f"rg:{cc}")
        e.wait("ssel")
        use("pdc", f"ytv:{cc}")
        for h in range(H):
            g = h // 2
            e.op("matmul", t.pdc[:, h * 64:(h + 1) * 64],
                 t.rg[:, g * 128:(g + 1) * 128],
                 t.ssl[:, (h % 2) * 512 + g * 64:(h % 2) * 512 + (g + 1) * 64],
                 start=True, stop=True,
                 mark=(f"corr:{cc}" if h == 15 else None))
        e.wait(f"gy:{cc}")
        use("pt", f"gytout:{cc}")
        for g in range(G):
            e.op("transpose", t.pt[:, g * 128:(g + 1) * 128],
                 t.gy[:, g * 128:(g + 1) * 128], t.ident[:, :],
                 mark=(f"gyt:{cc}" if g == G - 1 else None))
        e.wait(f"gytout:{cc}")
        e.wait("ld:wo")
        use("ph", f"x1:{cc}")
        for gi in range(G):
            for half in range(2):
                e.op("matmul", t.ph[:, half * 512:(half + 1) * 512],
                     t.gyt[:, gi * 128:(gi + 1) * 128],
                     t.wo[:, gi * 1024 + half * 512:gi * 1024 + half * 512 + 512],
                     start=(gi == 0), stop=(gi == 7),
                     mark=(f"wo:{cc}" if (gi == 7 and half == 1) else None))
        e.wait(f"xf:{cc}")
        use("pt", f"xftout:{cc}")
        for g in range(G):
            e.op("transpose", t.pt[:, g * 128:(g + 1) * 128],
                 t.xfb[:, g * 128:(g + 1) * 128], t.ident[:, :],
                 mark=(f"xft:{cc}" if g == G - 1 else None))
        if ci == 1:
            e.wait(f"xftout:{cc}")
            e.wait("ld:wfc3")
            for fg in range(FG):
                ps = "pha" if fg % 2 == 0 else "phb"
                use(ps, f"hrl:{ti}:{fg}")
                for gi in range(G):
                    e.op("matmul", t[ps][:, 0:256],
                         t.wfc[:, gi * 4096 + fg * 128:gi * 4096 + (fg + 1) * 128],
                         t.xft[:, gi * 256:(gi + 1) * 256],
                         start=(gi == 0), stop=(gi == 7),
                         mark=(f"hp:{ti}:{fg}" if gi == 7 else None))
            for c2 in (cc - 1, cc):
                ci2 = c2 % 2
                e.wait(f"hsq:{ti}:{FG - 1}")
                e.wait(f"ytv:{cc}")
                e.wait("ld:wpj3")
                use("pdc", f"ysb:{c2}")
                for fg in range(FG):
                    for half in range(2):
                        e.op("matmul", t.pdc[:, half * 512:(half + 1) * 512],
                             t.hb[:, fg * 256 + ci2 * 128:fg * 256 + ci2 * 128 + 128],
                             t.wpj[:, fg * 1024 + half * 512:fg * 1024 + half * 512 + 512],
                             start=(fg == 0), stop=(fg == 31),
                             mark=(f"y2:{c2}" if (fg == 31 and half == 1)
                                   else None))
    e.s.mark("endC:tensor", "TE")


def vector_C(e, t, p):
    for k in ("tensor", "scalar", "gpsimd"):
        e.wait(f"endA:{k}")
    e.wait("ld:sall")
    e.op("tensor_scalar", t.ssl[:, 0:512], t.sall[:, 0:512], t.sel[:, 0:1],
         None, OP.mult)
    for r in range(1, 8):
        e.drain_()
        e.op("scalar_tensor_tensor", t.ssl[:, 0:512],
             t.sall[:, r * 512:(r + 1) * 512], t.sel[:, r:r + 1],
             t.ssl[:, 0:512], OP.mult, OP.add)
    e.drain_()
    e.op("tensor_copy", t.ssl[64:128, 512:1024], t.ssl[64:128, 0:512])
    e.drain_()
    e.op("memset", t.ssl[64:128, 0:512], 0.0)
    e.op("memset", t.ssl[0:64, 512:1024], 0.0, mark="ssel")
    for cc in range(NCH):
        ci = cc % 2
        ti = cc // 2
        e.wait(f"cr:{cc}")
        e.wait(f"corr:{cc - 1}" if cc > 0 else "ssel")
        e.op("tensor_tensor", t.rg[:, :].rearrange("q (a b) -> q a b", a=G),
             t.rre[ci][:, :].rearrange("q (a b) -> q a b", a=G),
             t.eg[:, cc * 8:(cc + 1) * 8].unsqueeze(2)
             .broadcast_to([128, G, 128]), OP.mult, mark=f"rg:{cc}")
        if DBG in (True, "c", "c3") and cc == 0:
            e.drain_()
            e.op("tensor_copy", t.dbgrg[:, :], t.rg[:, :], mark="dbg:rg0")
        e.wait(f"corr:{cc}")
        e.wait(f"cy:{cc}")
        e.op("tensor_tensor", t.yt[:, :], t.pdc[:, :], t.yre[ci][:, :],
             OP.add, mark=f"ytv:{cc}")
        if DBG in (True, "c", "c3") and cc == 0:
            e.drain_()
            e.op("tensor_copy", t.dbgyt[:, :], t.yt[:, :], mark="dbg:yt0")
        e.wait(f"cg:{cc}")
        if cc > 0:
            e.wait(f"xft:{cc - 1}")
        e.drain_()
        e.op("tensor_tensor", t.gy[:, :], t.yt[:, :], t.gre[ci][:, :],
             OP.mult, mark=f"gy:{cc}")
        e.wait(f"gyt:{cc}")
        if cc > 0:
            e.wait(f"wo:{cc - 1}")
        e.op("tensor_copy", t.gyt[:, :],
             t.pt[:, :].rearrange("q (a b) -> q a b", a=G),
             mark=f"gytout:{cc}")
        e.wait(f"wo:{cc}")
        e.wait(f"cx:{cc}")
        if cc > 1:
            e.wait(f"outv:{cc - 2}")
        e.op("tensor_tensor", t.x1c[ci][:, :], t.ph[:, :], t.xc[ci][:, :],
             OP.add, mark=f"x1:{cc}")
        st = t.st3
        e.wait(f"l3acc:{cc}")
        e.op("scalar_tensor_tensor", st[:, 2:3], st[:, 0:1],
             1.0 / (1024.0 * 1024.0), st[:, 0:1], OP.mult, OP.mult)
        e.drain_()
        e.op("scalar_tensor_tensor", st[:, 5:6], st[:, 1:2], 1.0 / 1024.0,
             st[:, 2:3], OP.mult, OP.subtract, mark=f"l3var:{cc}")
        e.wait(f"l3s:{cc}")
        e.op("scalar_tensor_tensor", st[:, 4:5], st[:, 0:1], -1.0 / 1024.0,
             st[:, 3:4], OP.mult, OP.mult, mark=f"l3b:{cc}")
        e.wait(f"xft:{cc}")
        if ti > 0 and ci == 0:
            e.wait(f"hp:{ti - 1}:{FG - 1}")
        e.op("tensor_copy",
             bass.AP(t.xft, ci * 128, [[G * 256, 128], [256, G], [1, 128]]),
             t.pt[:, :].rearrange("q (a b) -> q a b", a=G),
             mark=f"xftout:{cc}")
        if ci == 1:
            for fg in range(FG):
                e.wait(f"hp:{ti}:{fg}")
                if fg > 1:
                    e.wait(f"hsq:{ti}:{fg - 2}")
                e.op("tensor_scalar", t.hrl[fg % 2][:, :],
                     (t.pha if fg % 2 == 0 else t.phb)[:, 0:256], 0.0, None,
                     OP.max, mark=f"hrl:{ti}:{fg}")
        if cc > 0:
            spike_V(e, t, cc - 1)
    spike_V(e, t, NCH - 1)
    e.s.mark("endC:vector", "VE")


def spike_V(e, t, cc):
    ci = cc % 2
    e.wait(f"ysb:{cc}")
    e.op("tensor_single_scalar", t.sk[:, :], t.ysb[:, :], 1.0, OP.is_gt,
         mark=f"spk0:{cc}")
    e.drain_()
    e.op("scalar_tensor_tensor", t.m[:, :], t.ysb[:, :], 1.9, t.sk[:, :],
         OP.mult, OP.subtract)
    for step in range(4):
        if step > 0:
            e.wait(f"u:{cc}:{step - 1}")
        e.drain_()
        e.op("tensor_single_scalar", t.sk[:, :], t.m[:, :], 1.0, OP.is_gt,
             mark=f"spks:{cc}:{step}")
        if step < 3:
            e.wait(f"gsum:{cc}:{step}" if step > 0 else f"gs0:{cc}")
            e.op("scalar_tensor_tensor", t.m[:, :], t.m[:, :], 0.9,
                 t.ysb[:, :], OP.mult, OP.add, mark=f"mq:{cc}:{step}")
    if cc > 1:
        e.wait(f"out:{cc - 2}")
    e.wait(f"gsum:{cc}:3")
    e.op("scalar_tensor_tensor", t.outst[ci][:, :], t.ssum[:, :], 0.2,
         t.x1c[ci][:, :], OP.mult, OP.add, mark=f"outv:{cc}")


def scalar_C(e, t, p):
    for k in ("tensor", "vector", "gpsimd"):
        e.wait(f"endA:{k}")
    for ti in range(NT):
        for ci, cc in enumerate((2 * ti, 2 * ti + 1)):
            e.wait(f"x1:{cc}")
            e.op("activation", t.dump[:, :], t.x1c[ci][:, :], AF.Copy,
                 accum_out=t.st3[:, 0:1])
            e.op("activation", t.dump[:, :], t.x1c[ci][:, :], AF.Square,
                 accum_out=t.st3[:, 1:2], mark=f"l3acc:{cc}")
            e.wait(f"l3var:{cc}")
            e.op("activation", t.st3[:, 3:4], t.st3[:, 5:6], AF.Ln, bias=t.epsc[:, 0:1])
            e.op("activation", t.st3[:, 3:4], t.st3[:, 3:4], AF.Exp,
                 scale=-0.5, mark=f"l3s:{cc}")
            e.wait(f"l3b:{cc}")
            e.wait(f"gyt:{cc}")
            e.op("activation", t.xfb[:, :], t.x1c[ci][:, :], AF.Identity,
                 scale=t.st3[:, 3:4], bias=t.st3[:, 4:5], mark=f"xf:{cc}")
        for fg in range(FG):
            e.wait(f"hrl:{ti}:{fg}")
            if ti > 0 and fg == 0:
                e.wait(f"y2:{2 * ti - 1}")
            e.op("activation", t.hb[:, fg * 256:(fg + 1) * 256],
                 t.hrl[fg % 2][:, :], AF.Square, mark=f"hsq:{ti}:{fg}")
        for ci, cc in enumerate((2 * ti, 2 * ti + 1)):
            e.wait(f"y2:{cc}")
            if cc > 0:
                e.wait(f"outv:{cc - 1}")
                e.wait(f"u:{cc - 1}:2")
            e.op("activation", t.ysb[:, :], t.pdc[:, :], AF.Copy,
                 mark=f"ysb:{cc}")
    e.s.mark("endC:scalar", "SE")


def gpsimd_C(e, t, p):
    for k in ("tensor", "vector", "scalar"):
        e.wait(f"endA:{k}")
    for cc in range(NCH):
        e.wait(f"spk0:{cc}")
        e.op("tensor_copy", t.ssum[:, :], t.sk[:, :], mark=f"gs0:{cc}")
        for step in range(4):
            e.wait(f"spks:{cc}:{step}")
            e.op("tensor_tensor", t.ssum[:, :], t.ssum[:, :], t.sk[:, :],
                 OP.add, mark=f"gsum:{cc}:{step}")
            if step < 3:
                e.wait(f"mq:{cc}:{step}")
                e.op("tensor_tensor", t.m[:, :], t.m[:, :], t.sk[:, :],
                     OP.subtract, mark=f"u:{cc}:{step}")
    e.s.mark("endC:gpsimd", "GP")


def sim_deadlock(s, phase):
    seqs = {k: list(v) for k, v in s.trace.items()}
    pos = {k: 0 for k in seqs}
    cnt = dict(s.base_counts)
    progress = True
    while progress:
        progress = False
        for k in seqs:
            while pos[k] < len(seqs[k]):
                it = seqs[k][pos[k]]
                if it[0] == "inc":
                    cnt[it[1]] = cnt.get(it[1], 0) + it[2]
                else:
                    semkey, v = s.marks[it[1]]
                    if semkey != {"tensor": "TE", "vector": "VE",
                                  "scalar": "SE", "gpsimd": "GP",
                                  "sync": "SYQ"}.get(k, "?") and                             cnt.get(semkey, 0) < v:
                        break
                pos[k] += 1
                progress = True
    stuck = {k: (pos[k], len(seqs[k])) for k in seqs if pos[k] < len(seqs[k])}
    if stuck:
        det = {}
        for k in stuck:
            it = s.trace[k][pos[k]]
            det[k] = (it, s.marks.get(it[1]) if it[0] == "waitm" else None,
                      {sk: cnt.get(sk, 0) for sk in ("TE", "VE", "SE", "GP",
                                                     "SYQ", "SPQ", "GPQ")})
        raise RuntimeError(f"DEADLOCK in phase {phase}: {stuck}\n{det}")


def build_nc():
    nc = bass.Bass("TRN2", target_bir_lowering=False,
                   dynamic_dma_scratch_size=4096)
    p = type("P", (), {})()
    p.dri = {}
    for name, sh, dt in (
        ("x", [TH, D_], F32), ("xprev", [1, D_], F32),
        ("flag", [128, 1], F32), ("sel", [128, 8], F32),
        ("gamma8", [128, 8], F32), ("beta8", [128, 8], F32),
        ("mixk8", [128, 8], F32), ("mixv8", [128, 8], F32),
        ("mixr8", [128, 8], F32), ("mixw8", [128, 8], F32),
        ("td8", [128, 8], F32), ("u8", [128, 8], F32),
        ("mask", [128, 512], BF16), ("bpair", [128, 2], BF16),
        ("ident", [128, 128], BF16), ("ones_f", [128, 128], F32),
        ("ones_b", [128, 128], BF16),
        ("wk", [D_, D_], BF16), ("wv", [D_, D_], BF16),
        ("wr", [D_, D_], BF16), ("ww", [D_, D_], BF16),
        ("wg", [D_, D_], BF16), ("wo", [D_, D_], BF16),
        ("wfc", [D_, FF], BF16), ("wpj", [FF, D_], BF16),
    ):
        p.dri[name] = nc.dram_tensor(name, sh, dt, kind="ExternalInput").ap()
    p.dro = nc.dram_tensor("out", [TH, D_], F32, kind="ExternalOutput").ap()
    p.y_sp = nc.dram_tensor("y_sp", [NCH, 128, D_], BF16, kind="Internal").ap()
    p.g_sp = nc.dram_tensor("g_sp", [NCH, 128, D_], BF16, kind="Internal").ap()
    p.r_sp = nc.dram_tensor("r_sp", [NCH, 128, D_], BF16, kind="Internal").ap()
    p.cc_in = nc.dram_tensor("cc_in", [128, 512], F32, kind="Internal").ap()
    p.cc_out = nc.dram_tensor("cc_out", [8, 128, 512], F32, kind="Internal",
                              addr_space="Shared").ap()
    p.dbg = {}
    if DBG:
        for nm, sh, dt in (
            ("ktil", [128, 2048], BF16), ("rhat", [128, 2048], BF16),
            ("vtm", [128, 2048], BF16), ("lin", [128, G * 258], F32),
            ("mxk", [128, 2048], BF16), ("mxw", [128, 2048], BF16),
            ("w1", [128, 2048], F32), ("fk", [128, 2048], BF16),
            ("fr", [128, 2048], BF16), ("msc", [128, 2048], BF16),
            ("xsb", [128, G * 257], BF16), ("cst", [128, 16], F32),
            ("ktm", [128, 1024], BF16), ("S2", [128, 512], F32),
            ("ysp", [NCH, 128, D_], BF16), ("gsp", [NCH, 128, D_], BF16),
            ("rsp", [NCH, 128, D_], BF16), ("Send", [128, 512], F32),
            ("x1", [128, 1024], F32), ("hb", [128, FG * 256], BF16),
            ("st1", [128, 8], F32), ("st2", [128, 8], F32),
            ("xh1", [128, 1024], F32), ("xh2", [128, 1024], BF16),
            ("xin", [128, 1024], F32),
            ("ssl", [128, 1024], BF16), ("sallo", [128, 4096], F32),
            ("eg", [128, 8 * NCH], F32), ("rgv", [128, 1024], BF16),
            ("ytv0", [128, 1024], BF16),
            ("rg0", [128, 1024], BF16), ("x10", [128, 1024], F32),
            ("yt0", [128, 1024], BF16),
        ):
            p.dbg[nm] = nc.dram_tensor("dbg_" + nm, sh, dt,
                                       kind="ExternalOutput").ap()

    s = Sched()
    with ExitStack() as PS:
        t = Tens(nc)
        for k in ("TE", "VE", "SE", "GP", "SYQ", "SPQ", "GPQ"):
            s.sems[k] = PS.enter_context(nc.semaphore(k))
        for name, sh, dt in (
            ("eg", [128, 8 * NCH], F32),
            ("ident", [128, 128], BF16),
            ("sel", [128, 8], F32), ("epsc", [128, 1], F32),
            ("st3", [128, 8], F32), ("dump", [128, 1024], F32),
        ):
            t.add(PS, name, sh, dt)

        with ExitStack() as SA:
            for name, sh, dt in (
                ("S", [128, 512], F32), ("sbf", [128, 1024], BF16),
                ("gpre", [128, 8], F32), ("elc", [128, 8], F32),
                ("ones_f", [128, 128], F32), ("ones_b", [128, 128], BF16),
                ("bpair", [128, 2], BF16), ("mask", [128, 512], BF16),
                ("flag", [128, 1], F32),
                ("gamma8", [128, 8], F32), ("beta8", [128, 8], F32),
                ("mixk8", [128, 8], F32), ("mixv8", [128, 8], F32),
                ("mixr8", [128, 8], F32), ("mixw8", [128, 8], F32),
                ("td8", [128, 8], F32), ("u8", [128, 8], F32),
                ("prevcol", [128, 8], F32), ("tiny", [1, 8], F32),
                ("st1", [128, 8], F32), ("st2", [128, 8], F32),
                ("sgk", [128, 8], F32), ("sgv", [128, 8], F32),
                ("sgr", [128, 8], F32), ("sgw", [128, 8], F32),
                ("bk8", [128, 8], F32), ("br8", [128, 8], F32),
                ("bw8", [128, 8], F32), ("tdb8", [128, 8], F32),
                ("beta_b", [128, 8], BF16),
                ("wk", [128, G * 1024], BF16), ("wv", [128, G * 1024], BF16),
                ("wr", [128, G * 1024], BF16), ("ww", [128, G * 1024], BF16),
                ("wg", [128, G * 1024], BF16),
                ("xin0", [128, 1024], F32), ("xin1", [128, 1024], F32),
                ("xh1", [128, 1024], F32), ("xh2", [128, 1024], BF16),
                ("xsb", [128, G * 257], BF16),
                ("tmx", [128, 2048], BF16),
                ("mxk", [128, 2048], BF16), ("mxv", [128, 2048], BF16),
                ("mxr", [128, 2048], BF16), ("mxw", [128, 2048], BF16),
                ("w1", [128, 2048], F32), ("expw", [128, 1024], F32),
                ("lin", [128, G * 258], F32),
                ("fr", [128, 2048], BF16), ("fk", [128, 2048], BF16),
                ("rsg", [128, 512], BF16),
                ("ktil", [128, 2048], BF16), ("ktil_e", [128, 2048], BF16),
                ("ktil_o", [128, 2048], BF16), ("ktm", [128, 1024], BF16),
                ("rhat", [128, 2048], BF16),
                ("vtm", [128, 2048], BF16),
                ("dcy", [128, 1024], BF16), ("ruk", [128, 1024], BF16),
                ("msc", [128, 2048], BF16),
                ("cst", [128, 16], F32), ("vc", [128, 1024], BF16),
                ("gst", [128, 1024], BF16),
                ("gsp", [128, 2048], BF16), ("ysp", [128, 2048], BF16),
                ("xprevpad", [128, 1024], BF16),
                ("bvfull", [128, 1024], BF16), ("bgfull", [128, 1024], BF16),
            ):
                t.add(SA, name, sh, dt)
            t.d["xin"] = [t.d["xin0"], t.d["xin1"]]
            t.alias("tinyb", "dump")
            t.alias("xprevrow", "xh1")
            t.d["bvrow"] = t.d["fr"][0:1, :]
            t.d["bgrow"] = t.d["fr"][0:1, :]
            t.addp(SA, "pst", [128, 1024], BF16)
            t.addp(SA, "ppa", [128, 512], F32)
            t.addp(SA, "ppb", [128, 512], F32)
            t.addp(SA, "ppv", [128, 1024], F32)
            t.addp(SA, "pca", [128, 512], F32)
            t.addp(SA, "pcb", [128, 512], F32)
            t.addp(SA, "pu", [128, 512], F32)

            emitters_A = (("sync", sync_A), ("tensor", tensor_A),
                          ("vector", vector_A), ("scalar", scalar_A),
                          ("gpsimd", gpsimd_A))
            s.start(True)
            s.base_counts = {}
            for key, fn in emitters_A:
                fn(Eng(s, key, None), t, p)
            countsA = dict(s.counts)
            sim_deadlock(s, "A")
            traceA = {k: list(v) for k, v in s.trace.items()}
            s.start(False)
            real = {"sync": nc.sync, "tensor": nc.tensor, "vector": nc.vector,
                    "scalar": nc.scalar, "gpsimd": nc.gpsimd}
            with nc.Block() as blk:
                for key, fn in emitters_A:
                    getattr(blk, key)(lambda eng, fn=fn, key=key:
                                      fn(Eng(s, key, eng), t, p))
            countsA2 = dict(s.counts)
            assert countsA == countsA2, (countsA, countsA2)

        if not DO_C:
            return nc
        with ExitStack() as SC:
            for name, sh, dt in (
                ("wfc", [128, FG * 1024], BF16), ("wpj", [128, FG * 1024], BF16),
                ("wo", [128, G * 1024], BF16), ("ssl", [128, 1024], BF16),
                ("xc0", [128, 1024], F32), ("xc1", [128, 1024], F32),
                ("yre0", [128, 1024], BF16), ("yre1", [128, 1024], BF16),
                ("gre0", [128, 1024], BF16), ("gre1", [128, 1024], BF16),
                ("rre0", [128, 1024], BF16), ("rre1", [128, 1024], BF16),
                ("rg", [128, 1024], BF16),
                ("gy", [128, 1024], BF16), ("gyt", [128, 1024], BF16),
                ("x10", [128, 1024], F32), ("x11", [128, 1024], F32),
                ("xft", [128, 2048], BF16),
                ("hb", [128, FG * 256], BF16),
                ("hrl0", [128, 256], BF16), ("hrl1", [128, 256], BF16),
                ("ysb", [128, 1024], F32), ("m", [128, 1024], F32),
                ("sk", [128, 1024], BF16), ("ssum", [128, 1024], BF16),
            ):
                t.add(SC, name, sh, dt)
            t.d["sall"] = t.d["hb"][:, :].bitcast(F32)
            t.d["x1c"] = [t.d["x10"], t.d["x11"]]
            t.d["dbgrg"] = t.d["m"][:, 0:512].bitcast(BF16)
            t.d["dbgyt"] = t.d["m"][:, 512:1024].bitcast(BF16)
            t.d["xc"] = [t.d["xc0"], t.d["xc1"]]
            t.d["yre"] = [t.d["yre0"], t.d["yre1"]]
            t.d["gre"] = [t.d["gre0"], t.d["gre1"]]
            t.d["rre"] = [t.d["rre0"], t.d["rre1"]]
            t.d["hrl"] = [t.d["hrl0"], t.d["hrl1"]]
            t.d["outst"] = [t.d["xc0"], t.d["xc1"]]
            t.alias("yt", "rg")
            t.alias("xfb", "gy")
            t.addp(SC, "pt", [128, 1024], BF16)
            t.addp(SC, "pdc", [128, 1024], F32)
            t.addp(SC, "ph", [128, 1024], F32)
            t.addp(SC, "pha", [128, 256], F32)
            t.addp(SC, "phb", [128, 256], F32)

            emitters_C = (("sync", sync_C), ("tensor", tensor_C),
                          ("vector", vector_C), ("scalar", scalar_C),
                          ("gpsimd", gpsimd_C))
            saved = dict(s.counts)
            s.rec = True
            s.trace = {k: list(v) for k, v in traceA.items()}
            for key, fn in emitters_C:
                fn(Eng(s, key, None), t, p)
            countsC = dict(s.counts)
            s.base_counts = {}
            sim_deadlock(s, "C")
            s.rec = False
            s.counts = saved
            with nc.Block() as blk:
                for key, fn in emitters_C:
                    getattr(blk, key)(lambda eng, fn=fn, key=key:
                                      fn(Eng(s, key, eng), t, p))
            assert s.counts == countsC, (s.counts, countsC)
    return nc


_NC_CACHE = []


def kernel(**inputs):
    xs = np.ascontiguousarray(np.asarray(inputs["L0_attn_x"], np.float32))
    gamma = np.asarray(inputs["L0_attn_ln_w"], np.float32).reshape(D_)
    beta = np.asarray(inputs["L0_attn_ln_b"], np.float32).reshape(D_)
    mixk = np.asarray(inputs["L0_attn_mix_k"], np.float32).reshape(D_)
    mixv = np.asarray(inputs["L0_attn_mix_v"], np.float32).reshape(D_)
    mixr = np.asarray(inputs["L0_attn_mix_r"], np.float32).reshape(D_)
    mixw = np.asarray(inputs["L0_attn_mix_w"], np.float32).reshape(D_)
    td = np.asarray(inputs["L0_attn_time_decay"], np.float32).reshape(D_)
    tf = np.asarray(inputs["L0_attn_time_first"], np.float32).reshape(D_)

    def col8(v):
        return np.ascontiguousarray(v.reshape(8, 128).T)

    wT = {k: np.ascontiguousarray(
        np.asarray(inputs[f"L0_attn_W{k}"], np.float32).T).astype(BF)
        for k in ("r", "k", "v", "w", "g", "o")}
    wfcT = np.ascontiguousarray(
        np.asarray(inputs["L0_ffn_W_fc"], np.float32).T).astype(BF)
    wpjT = np.ascontiguousarray(
        np.asarray(inputs["L0_ffn_W_proj"], np.float32).T).astype(BF)

    mask = np.zeros((128, 512), np.float32)
    for blkk in range(4):
        for ss in range(128):
            mask[ss, blkk * 128 + ss + 1:(blkk + 1) * 128] = 1.0
    bpair = np.zeros((128, 2), np.float32)
    bpair[0:64, 0] = 1.0
    bpair[64:128, 1] = 1.0
    common = {
        "gamma8": col8(gamma), "beta8": col8(beta),
        "mixk8": col8(mixk), "mixv8": col8(mixv),
        "mixr8": col8(mixr), "mixw8": col8(mixw),
        "td8": col8(td), "u8": col8(tf),
        "mask": mask.astype(BF), "bpair": bpair.astype(BF),
        "ident": np.eye(128, dtype=np.float32).astype(BF),
        "ones_f": np.ones((128, 128), np.float32),
        "ones_b": np.ones((128, 128), np.float32).astype(BF),
        "wk": wT["k"], "wv": wT["v"], "wr": wT["r"], "ww": wT["w"],
        "wg": wT["g"], "wo": wT["o"], "wfc": wfcT, "wpj": wpjT,
    }
    in_maps = []
    for c in range(8):
        b, half = c // 2, c % 2
        m = dict(common)
        m["x"] = np.ascontiguousarray(xs[b, half * TH:(half + 1) * TH, :])
        m["xprev"] = (np.ascontiguousarray(xs[b, half * TH - 1:half * TH, :])
                      if half else np.zeros((1, D_), np.float32))
        m["flag"] = np.full((128, 1), float(half), np.float32)
        sel = np.zeros((128, 8), np.float32)
        if half == 1:
            sel[:, c - 1] = 1.0
        m["sel"] = sel
        in_maps.append(m)

    if not _NC_CACHE:
        _NC_CACHE.append(build_nc())
    nc = _NC_CACHE[0]
    res = run_bass_kernel_spmd(nc, in_maps, core_ids=list(range(8)))
    out = np.empty((B_, T_, D_), np.float32)
    for c in range(8):
        b, half = c // 2, c % 2
        out[b, half * TH:(half + 1) * TH, :] = res.results[c]["out"]
    return out


# version 10
# speedup vs baseline: 1.0753x; 1.0753x over previous
import numpy as np
import ml_dtypes
from contextlib import ExitStack

import concourse.bass as bass
import concourse.mybir as mybir
from concourse.bass_utils import run_bass_kernel_spmd

BF = ml_dtypes.bfloat16
F32 = mybir.dt.float32
BF16 = mybir.dt.bfloat16
AF = mybir.ActivationFunctionType
OP = mybir.AluOpType
AX = mybir.AxisListType.X

B_, T_, D_ = 4, 8192, 1024
TH = 4096
CH = 128
NCH = TH // CH
TPT = 256
NT = TH // TPT
G = 8
H = 16
FF = 4096
FG = 32
EPS = 1e-5
LN30 = float(np.log(30.0))

EVEN_ODD = [0, 2, 4, 6, 8, 10, 12, 14, 1, 3, 5, 7, 9, 11, 13, 15]
DO_C = True
DBG = False
DO_SETUPB = True
DO_B = True


class Sched:
    def __init__(self):
        self.marks = {}
        self.counts = {}
        self.rec = True
        self.sems = {}

    def start(self, rec):
        self.rec = rec
        self.counts = {}
        self.trace = {}

    def bump(self, k, amt):
        self.counts[k] = self.counts.get(k, 0) + amt
        return self.counts[k]

    def mark(self, name, semkey):
        v = self.counts.get(semkey, 0)
        if self.rec:
            assert name not in self.marks, f"dup mark {name}"
            self.marks[name] = (semkey, v)
        else:
            assert self.marks[name] == (semkey, v), \
                f"mark drift {name}: {self.marks[name]} vs {(semkey, v)}"


class Eng:
    def __init__(self, sched, key, real):
        self.s = sched
        self.key = key
        self.real = real
        self.csem = {"tensor": "TE", "vector": "VE", "scalar": "SE",
                     "gpsimd": "GP"}.get(key)
        self.dsem = {"sync": "SYQ", "scalar": "SPQ", "gpsimd": "GPQ"}.get(key)
        self.hw = {}

    def op(self, method, *args, mark=None, **kw):
        doinc = mark is not None
        if self.s.rec and doinc:
            self.s.trace.setdefault(self.key, []).append(
                ("inc", self.csem, 1))
        if not self.s.rec:
            inst = getattr(self.real, method)(*args, **kw)
            if doinc:
                inst.then_inc(self.s.sems[self.csem], 1)
        if doinc:
            self.s.bump(self.csem, 1)
        if mark:
            self.s.mark(mark, self.csem)

    def dma(self, out, in_, mark=None, paced=True, **kw):
        if self.s.rec:
            self.s.trace.setdefault(self.key, []).append(
                ("inc", self.dsem, 16))
        self.s.bump(self.dsem, 16)
        if not self.s.rec:
            inst = self.real.dma_start(out=out, in_=in_, **kw)
            inst.then_inc(self.s.sems[self.dsem], 16)
            if paced:
                self.real.wait_ge(self.s.sems[self.dsem],
                                  self.s.counts[self.dsem])
        if mark:
            self.s.mark(mark, self.dsem)

    def fence(self):
        if not self.s.rec:
            self.real.wait_ge(self.s.sems[self.dsem],
                              self.s.counts[self.dsem])

    def mark_dma(self, name):
        self.s.mark(name, self.dsem)

    def wait(self, markname):
        if markname is None:
            return
        if self.s.rec:
            self.s.trace.setdefault(self.key, []).append(("waitm", markname))
            return
        semkey, v = self.s.marks[markname]
        if semkey == self.csem or v == 0:
            return
        if self.hw.get(semkey, 0) >= v:
            return
        self.real.wait_ge(self.s.sems[semkey], v)
        self.hw[semkey] = v

    def drain_(self):
        if not self.s.rec:
            self.real.drain()

    def wait_all_dma(self):
        v = self.s.counts.get(self.dsem, 0)
        if not self.s.rec:
            self.real.wait_ge(self.s.sems[self.dsem], v)


class Tens:
    def __init__(self, nc):
        self.nc = nc
        self.d = {}

    def add(self, stack, name, shape, dt):
        self.d[name] = stack.enter_context(
            self.nc.sbuf_tensor("sb_" + name, shape, dt))

    def addp(self, stack, name, shape, dt):
        self.d[name] = stack.enter_context(
            self.nc.psum_tensor("ps_" + name, shape, dt))

    def alias(self, name, other):
        self.d[name] = self.d[other]

    def __getattr__(self, n):
        if n in ("nc", "d"):
            raise AttributeError(n)
        return self.d[n]

    def __getitem__(self, n):
        return self.d[n]


def sync_A(e, t, p):
    for name in ("gamma8", "beta8", "mixk8", "mixv8", "mixr8", "mixw8",
                 "td8", "u8", "flag", "sel", "mask", "bpair", "ident",
                 "ones_f", "ones_b"):
        e.dma(t[name][:, :], p.dri[name])
    e.dma(t.xprevrow[0:1, :], p.dri["xprev"])
    for w in ("wk", "wv", "wr", "ww", "wg"):
        e.dma(t[w][:, :].rearrange("q (a d) -> q a d", a=G),
              p.dri[w].rearrange("(a q) d -> q a d", q=128),
              mark=f"ld:{w}")
    e.s.mark("ld:setup", "SYQ")
    for ti in range(NT):
        for cc in (2 * ti, 2 * ti + 1):
            if cc >= 2:
                e.wait(f"xdone:{cc - 2}")
            e.dma(t.xin[cc % 2][:, :],
                  p.dri["x"][cc * CH:(cc + 1) * CH, :], mark=f"x:{cc}")
        if ti == 1 and DBG in (True, "a"):
            for nm, mk in (("ktil", "kev:0:3"), ("rhat", "rhat:0"),
                           ("vtm", "vev:1"), ("lin", "scan:1"),
                           ("mxk", "mix:k:0"), ("mxw", "mix:w:0"),
                           ("w1", "w1:0"), ("fk", "fk:0"), ("fr", "fr:0"),
                           ("msc", "mev:1:3"), ("xsb", "xsbc:1"),
                           ("cst", "cout:0"), ("ktm", "ktmv:0")):
                e.wait(mk)
                e.dma(p.dbg[nm], t[nm][:, :])
            e.wait("supd:1")
            e.dma(p.dbg["S2"], t.S[:, :])
        if ti >= 1:
            spills(e, t, p, ti - 1)
    spills(e, t, p, NT - 1)
    if DO_B and DO_C:
        e.wait(f"supd:{NCH - 1}")
        e.dma(p.cc_in[:, :], t.S[:, :], mark="cc:S")


def spills(e, t, p, ti):
    for ci, cc in enumerate((2 * ti, 2 * ti + 1)):
        e.wait(f"yev:{cc}")
        e.dma(p.y_sp[cc, :, :], t.ysp[:, ci * 1024:(ci + 1) * 1024],
              mark=f"spy:{cc}")
        e.wait(f"g:{cc}")
        e.dma(p.g_sp[cc, :, :], t.gsp[:, ci * 1024:(ci + 1) * 1024],
              mark=f"spg:{cc}")
    e.wait(f"rhat:{ti}")
    for ci, cc in enumerate((2 * ti, 2 * ti + 1)):
        e.dma(p.r_sp[cc, :, :],
              bass.AP(t.rhat, ci * 128, [[2048, 128], [256, G], [1, 128]]),
              mark=f"spr:{cc}")


def tensor_A(e, t, p):
    ID = t.ident[:, :]
    g_ = {}

    def use(ps, freemark):
        e.wait(g_.get(ps))
        g_[ps] = freemark

    e.wait("ld:setup")
    for w, mk in ((("wk", "bk"), ("wr", "br"), ("ww", "bw")) if DO_SETUPB
                  else ()):
        e.wait(f"ld:{w}")
        e.wait("su:betab")
        use("ppa", f"su:{mk}out")
        for dg in range(G):
            for gi in range(G):
                e.op("matmul", t.ppa[:, dg:dg + 1],
                     t[w][:, gi * 1024 + dg * 128:gi * 1024 + (dg + 1) * 128],
                     t.beta_b[:, gi:gi + 1], start=(gi == 0), stop=(gi == 7),
                     mark=(f"su:{mk}" if (dg == G - 1 and gi == 7) else None))
    for w, mk in ((("wv", "bv"), ("wg", "bg")) if DO_SETUPB else ()):
        e.wait(f"ld:{w}")
        for half in range(2):
            use("ppb", f"su:{mk}rowout{half}")
            for gi in range(G):
                e.op("matmul", t.ppb[0:1, 0:512], t.beta_b[:, gi:gi + 1],
                     t[w][:, gi * 1024 + half * 512:gi * 1024 + half * 512 + 512],
                     start=(gi == 0), stop=(gi == 7),
                     mark=(f"su:{mk}row{half}" if gi == 7 else None))
            e.wait(f"su:{mk}rowout{half}")
        use("ppv", f"su:{mk}full")
        for half in range(2):
            e.op("matmul", t.ppv[:, half * 512:(half + 1) * 512],
                 t.ones_b[0:1, :],
                 t[mk + "row"][0:1, half * 512:(half + 1) * 512],
                 start=True, stop=True,
                 mark=(f"su:{mk}bc" if half == 1 else None))
    if DO_SETUPB:
        e.wait("su:xprevpad")
        use("pst", "su:prevcol0")
        for g in range(G):
            e.op("transpose", t.pst[:, g * 128:(g + 1) * 128],
                 t.xprevpad[:, g * 128:(g + 1) * 128], ID,
                 mark=("su:pxt" if g == G - 1 else None))
    e.wait("su:fold")

    for ti in range(NT):
        ccs = (2 * ti, 2 * ti + 1)
        for ci, cc in enumerate(ccs):
            e.wait(f"xh:{cc}")
            use("pst", f"xsbc:{cc}")
            for g in range(G):
                e.op("transpose", t.pst[:, g * 128:(g + 1) * 128],
                     t.xh2[:, g * 128:(g + 1) * 128], ID,
                     mark=(f"xt:{cc}" if g == G - 1 else None))
        for w, nm in (("ww", "w"), ("wk", "k"), ("wr", "r")):
            e.wait(f"mix:{nm}:{ti}")
            for dp in range(4):
                ps = "ppa" if dp % 2 == 0 else "ppb"
                free = {"w": f"wev:{ti}:{dp}", "k": f"kevp:{ti}:{dp}",
                        "r": f"rsig:{ti}:{dp}"}[nm]
                use(ps, free)
                for sub in range(2):
                    dg = dp * 2 + sub
                    for gi in range(G):
                        e.op("matmul",
                             t[ps][:, sub * 256:(sub + 1) * 256],
                             t[w][:, gi * 1024 + dg * 128:gi * 1024 + (dg + 1) * 128],
                             t["mx" + nm][:, gi * 256:(gi + 1) * 256],
                             start=(gi == 0), stop=(gi == 7),
                             mark=(f"{nm}p:{ti}:{dp}" if (sub == 1 and gi == 7)
                                   else None))
        for ci, cc in enumerate(ccs):
            for w, nm, mxn in (("wv", "v", "mxv"), ("wg", "g", "mxr")):
                e.wait(f"mix:{'v' if nm == 'v' else 'r'}:{ti}")
                use("ppv", f"{nm}ev:{cc}")
                for gi in range(G):
                    for half in range(2):
                        e.op("matmul", t.ppv[:, half * 512:(half + 1) * 512],
                             t[mxn][:, gi * 256 + ci * 128:gi * 256 + ci * 128 + 128],
                             t[w][:, gi * 1024 + half * 512:gi * 1024 + half * 512 + 512],
                             start=(gi == 0), stop=(gi == 7),
                             mark=(f"{nm}p:{cc}" if (gi == 7 and half == 1)
                                   else None))
        for ci, cc in enumerate(ccs):
            if not DO_B:
                continue
            e.wait(f"kev:{ti}:3")
            use("pst", f"ktmv:{cc}")
            for g in range(G):
                e.op("transpose", t.pst[:, g * 128:(g + 1) * 128],
                     t.ktil[:, g * 256 + ci * 128:g * 256 + ci * 128 + 128], ID,
                     mark=(f"ktm:{cc}" if g == G - 1 else None))
            e.wait(f"rhat:{ti}")
            for wv in range(4):
                hs = [(wv // 2) * 8 + (wv % 2) + 2 * i for i in range(4)]
                ps = "pca" if wv % 2 == 0 else "pcb"
                use(ps, f"mev:{cc}:{wv}")
                for i, h in enumerate(hs):
                    g = h // 2
                    ksrc = t.ktil_e if h % 2 == 0 else t.ktil_o
                    e.op("matmul", t[ps][:, i * 128:(i + 1) * 128],
                         ksrc[:, g * 256 + ci * 128:g * 256 + ci * 128 + 128],
                         t.rhat[:, g * 256 + ci * 128:g * 256 + ci * 128 + 128],
                         start=True, stop=True,
                         mark=(f"pw:{cc}:{wv}" if i == 3 else None))
            e.wait(f"mev:{cc}:3")
            e.wait(f"vev:{cc}")
            e.wait(f"gev:{cc}")
            use("ppv", f"yev:{cc}")
            if cc > 0:
                e.wait(f"sbf:{cc - 1}")
            for bank in range(2):
                for h in range(bank * 8, bank * 8 + 8):
                    e.op("matmul", t.ppv[:, h * 64:(h + 1) * 64],
                         t.msc[:, h * 128:(h + 1) * 128],
                         t.vtm[:, ci * 1024 + h * 64:ci * 1024 + (h + 1) * 64],
                         start=(h % 8 == 0), stop=False)
                for h in range(bank * 8, bank * 8 + 8):
                    g = h // 2
                    e.op("matmul", t.ppv[:, h * 64:(h + 1) * 64],
                         t.rhat[:, g * 256 + ci * 128:g * 256 + ci * 128 + 128],
                         t.sbf[:, (h % 2) * 512 + g * 64:(h % 2) * 512 + (g + 1) * 64],
                         start=False, stop=(h % 8 == 7),
                         mark=(f"ycross:{cc}" if h == 15 else None))
            e.wait(f"ktmv:{cc}")
            use("pu", f"supd:{cc}")
            for g in range(G):
                for sub in range(2):
                    h = 2 * g + sub
                    e.op("matmul",
                         t.pu[sub * 64:(sub + 1) * 64, g * 64:(g + 1) * 64],
                         t.ktm[:, h * 64:(h + 1) * 64],
                         t.vtm[:, ci * 1024 + h * 64:ci * 1024 + (h + 1) * 64],
                         start=True, stop=True, tile_position=(0, sub * 64),
                         mark=(f"uu:{cc}" if (g == G - 1 and sub == 1) else None))
            e.wait(f"ruk:{cc}")
            use("ppa", f"cout:{cc}")
            for g in range(G):
                e.op("matmul", t.ppa[:, 2 * g:2 * g + 2],
                     t.ruk[:, g * 128:(g + 1) * 128], t.bpair[:, :],
                     start=True, stop=True,
                     mark=(f"c:{cc}" if g == G - 1 else None))
    e.s.mark("endA:tensor", "TE")


def vector_A(e, t, p):
    e.wait("ld:setup")
    e.op("memset", t.epsc[:, :], EPS, mark="su:eps")
    e.op("memset", t.S[:, :], 0.0)
    e.op("memset", t.sbf[:, :], 0.0)
    e.op("memset", t.ktil_e[:, :], 0.0)
    e.op("memset", t.ktil_o[:, :], 0.0)
    e.op("memset", t.gpre[:, :], 0.0)
    e.op("memset", t.lin[:, :], 0.0)
    e.op("memset", t.xprevpad[:, :], 0.0)
    e.op("tensor_copy", t.beta_b[:, :], t.beta8[:, :], mark="su:betab")
    if DO_SETUPB:
        for mk, dst in (("bk", "bk8"), ("br", "br8"), ("bw", "bw8")):
            e.wait(f"su:{mk}")
            e.op("tensor_copy", t[dst][:, :], t.ppa[:, 0:8], mark=f"su:{mk}out")
        for mk in ("bv", "bg"):
            for half in range(2):
                e.wait(f"su:{mk}row{half}")
                e.op("tensor_copy",
                     t[mk + "row"][0:1, half * 512:(half + 1) * 512],
                     t.ppb[0:1, 0:512], mark=f"su:{mk}rowout{half}")
            e.wait(f"su:{mk}bc")
            e.op("tensor_copy", t[mk + "full"][:, :], t.ppv[:, :],
                 mark=f"su:{mk}full")
    else:
        for dst in ("bk8", "br8", "bw8"):
            e.op("memset", t[dst][:, :], 0.0, mark=f"su:{dst[:2]}out")
        e.op("memset", t.bvfull[:, :], 0.0, mark="su:bvfull")
        e.op("memset", t.bgfull[:, :], 0.0, mark="su:bgfull")
    e.wait("su:bwout")
    e.drain_()
    e.op("tensor_tensor", t.tdb8[:, :], t.bw8[:, :], t.td8[:, :], OP.add,
         mark="su:tdb")
    for w in ("wk", "wv", "wr", "ww", "wg"):
        for gi in range(G):
            e.op("tensor_scalar", t[w][:, gi * 1024:(gi + 1) * 1024],
                 t[w][:, gi * 1024:(gi + 1) * 1024],
                 t.gamma8[:, gi:gi + 1], None, OP.mult,
                 mark=("su:fold" if (w == "wg" and gi == G - 1) else None))
    xp = t.xprevrow
    for it in range(2):
        e.op("reduce_sum", t.tiny[0:1, 0:1], xp[0:1, :], AX)
        e.op("scalar_tensor_tensor", t.tinyb[0:1, :], xp[0:1, :], 1.0,
             xp[0:1, :], OP.bypass, OP.mult, accum_out=t.tiny[0:1, 1:2])
        e.drain_()
        e.op("tensor_scalar", t.tiny[0:1, 0:2], t.tiny[0:1, 0:2],
             1.0 / 1024.0, None, OP.mult)
        e.drain_()
        e.op("tensor_tensor", t.tiny[0:1, 2:3], t.tiny[0:1, 0:1],
             t.tiny[0:1, 0:1], OP.mult)
        e.drain_()
        e.op("tensor_tensor", t.tiny[0:1, 2:3], t.tiny[0:1, 1:2],
             t.tiny[0:1, 2:3], OP.subtract)
        e.drain_()
        e.op("tensor_scalar", t.tiny[0:1, 2:3], t.tiny[0:1, 2:3], EPS, None,
             OP.add, mark=f"su:xpv{it}")
        e.wait(f"su:xps{it}")
        e.drain_()
        e.op("tensor_scalar", xp[0:1, :], xp[0:1, :],
             t.tiny[0:1, 0:1], t.tiny[0:1, 4:5], OP.subtract, OP.mult)
        e.drain_()
    e.op("tensor_copy", t.xprevpad[0:1, :], xp[0:1, :], mark="su:xprevpad")
    if DO_SETUPB:
        e.wait("su:pxt")
        for g in range(G):
            e.op("tensor_copy", t.prevcol[:, g:g + 1],
                 t.pst[:, g * 128:g * 128 + 1])
        e.drain_()
        e.op("tensor_scalar", t.prevcol[:, :], t.prevcol[:, :],
             t.flag[:, 0:1], None, OP.mult, mark="su:prevcol0")
    else:
        e.op("memset", t.prevcol[:, :], 0.0, mark="su:prevcol0")

    for ti in range(NT):
        ccs = (2 * ti, 2 * ti + 1)
        for ci, cc in enumerate(ccs):
            for ln in (1, 2):
                st = t["st%d" % ln]
                e.wait(f"l{ln}acc:{cc}")
                e.op("scalar_tensor_tensor", st[:, 2:3], st[:, 0:1],
                     1.0 / (1024.0 * 1024.0), st[:, 0:1], OP.mult, OP.mult)
                e.drain_()
                e.op("scalar_tensor_tensor", st[:, 5:6], st[:, 1:2],
                     1.0 / 1024.0, st[:, 2:3], OP.mult, OP.subtract,
                     mark=f"l{ln}var:{cc}")
                e.wait(f"l{ln}s:{cc}")
                e.op("scalar_tensor_tensor", st[:, 4:5], st[:, 0:1],
                     -1.0 / 1024.0, st[:, 3:4], OP.mult, OP.mult,
                     mark=f"l{ln}b:{cc}")
        if ti == 0:
            e.wait("su:prevcol0")
            e.op("tensor_copy",
                 bass.AP(t.xsb, 0, [[G * 257, 128], [257, G], [1, 1]]),
                 t.prevcol[:, :].unsqueeze(2))
        else:
            e.wait(f"gmix:w:{ti - 1}")
            e.op("tensor_copy",
                 bass.AP(t.xsb, 0, [[G * 257, 128], [257, G], [1, 1]]),
                 bass.AP(t.xsb, 256, [[G * 257, 128], [257, G], [1, 1]]))
        for ci, cc in enumerate(ccs):
            e.wait(f"xt:{cc}")
            e.op("tensor_copy",
                 bass.AP(t.xsb, 1 + ci * 128,
                         [[G * 257, 128], [257, G], [1, 128]]),
                 t.pst[:, :].rearrange("q (a b) -> q a b", a=G),
                 mark=f"xsbc:{cc}")
        xcur = bass.AP(t.xsb, 1, [[G * 257, 128], [257, G], [1, 256]])
        xprv = bass.AP(t.xsb, 0, [[G * 257, 128], [257, G], [1, 256]])
        e.drain_()
        e.op("tensor_tensor", t.tmx[:, :].rearrange("q (a b) -> q a b", a=G),
             xcur, xprv, OP.subtract, mark=f"tmix:{ti}")
        for nm in ("k", "v", "r", "w"):
            e.wait(f"gmix:{nm}:{ti}")
            e.op("tensor_tensor",
                 t["mx" + nm][:, :].rearrange("q (a b) -> q a b", a=G),
                 t["mx" + nm][:, :].rearrange("q (a b) -> q a b", a=G),
                 xprv, OP.add, mark=f"mix:{nm}:{ti}")
        if ti > 0:
            e.wait(f"expw:{2 * ti - 1}")
        for dp in range(4):
            ps = t.ppa if dp % 2 == 0 else t.ppb
            e.wait(f"wp:{ti}:{dp}")
            for sub in range(2):
                dg = dp * 2 + sub
                e.op("tensor_scalar", t.w1[:, dg * 256:(dg + 1) * 256],
                     ps[:, sub * 256:(sub + 1) * 256],
                     t.tdb8[:, dg:dg + 1], LN30, OP.add, OP.min,
                     mark=(f"wev:{ti}:{dp}" if sub == 1 else None))
        e.drain_()
        e.op("tensor_single_scalar", t.w1[:, :], t.w1[:, :], -10.0, OP.max,
             mark=f"w1:{ti}")
        for ci, cc in enumerate(ccs):
            e.wait(f"expw:{cc}")
            if ti > 0 and ci == 0:
                e.wait(f"fk:{ti - 1}")
                e.wait(f"elc:{2 * ti - 1}")
                e.wait(f"eg:{2 * ti - 1}")
            for g in range(G):
                e.op("tensor_tensor_scan",
                     t.lin[:, g * 258 + ci * 129 + 1:g * 258 + ci * 129 + 129],
                     t.ones_f[:, 0:128], t.expw[:, g * 128:(g + 1) * 128],
                     0.0, OP.mult, OP.subtract,
                     mark=(f"scan:{cc}" if g == G - 1 else None))
        for dp in range(4):
            ps = t.ppa if dp % 2 == 0 else t.ppb
            e.wait(f"kp:{ti}:{dp}")
            e.wait(f"fk:{ti}")
            if ti > 0 and dp == 0 and DO_B:
                e.wait(f"pw:{2 * ti - 1}:3")
                e.wait(f"ktm:{2 * ti - 1}")
                e.wait(f"rk:{2 * ti - 1}")
            for sub in range(2):
                dg = dp * 2 + sub
                e.op("scalar_tensor_tensor",
                     t.ktil[:, dg * 256:(dg + 1) * 256],
                     ps[:, sub * 256:(sub + 1) * 256], t.bk8[:, dg:dg + 1],
                     t.fk[:, dg * 256:(dg + 1) * 256], OP.add, OP.mult,
                     mark=(f"kevp:{ti}:{dp}" if sub == 1 else None))
        e.drain_()
        for dp in range(4):
            for sub in range(2):
                dg = dp * 2 + sub
                e.op("tensor_copy", t.ktil_e[0:64, dg * 256:(dg + 1) * 256],
                     t.ktil[0:64, dg * 256:(dg + 1) * 256])
                e.op("tensor_copy", t.ktil_o[64:128, dg * 256:(dg + 1) * 256],
                     t.ktil[64:128, dg * 256:(dg + 1) * 256],
                     mark=(f"kev:{ti}:{dp}" if sub == 1 else None))
        for dp in range(4):
            e.wait(f"rsig:{ti}:{dp}")
            e.wait(f"fr:{ti}")
            if ti > 0 and dp == 0:
                if DO_B:
                    e.wait(f"ycross:{2 * ti - 1}")
                e.wait(f"spr:{2 * ti - 1}")
            e.op("tensor_tensor", t.rhat[:, dp * 512:(dp + 1) * 512],
                 t.rsg[:, :], t.fr[:, dp * 512:(dp + 1) * 512], OP.mult,
                 mark=(f"rhat:{ti}" if dp == 3 else f"rhev:{ti}:{dp}"))
        for ci, cc in enumerate(ccs):
            e.wait(f"vp:{cc}")
            if cc > 1 and DO_B:
                e.wait(f"uu:{cc - 2}")
            e.op("scalar_tensor_tensor", t.vtm[:, ci * 1024:(ci + 1) * 1024],
                 t.ppv[:, :], 1.0, t.bvfull[:, :], OP.bypass, OP.add,
                 mark=f"vev:{cc}")
            e.wait(f"gp:{cc}")
            if cc > 0:
                e.wait(f"g:{cc - 1}")
            e.op("scalar_tensor_tensor", t.gst[:, :], t.ppv[:, :], 1.0,
                 t.bgfull[:, :], OP.bypass, OP.add, mark=f"gev:{cc}")
        for ci, cc in enumerate(ccs):
            if not DO_B:
                e.op("memset", t.ysp[:, ci * 1024:(ci + 1) * 1024], 0.0,
                     mark=f"yev:{cc}")
                continue
            e.wait(f"ktm:{cc}")
            if cc > 0:
                e.wait(f"uu:{cc - 1}")
            e.op("tensor_copy", t.ktm[:, :],
                 t.pst[:, :].rearrange("q (a b) -> q a b", a=G),
                 mark=f"ktmv:{cc}")
            for wv in range(4):
                ps = t.pca if wv % 2 == 0 else t.pcb
                hs = [(wv // 2) * 8 + (wv % 2) + 2 * i for i in range(4)]
                e.wait(f"pw:{cc}:{wv}")
                if cc > 0 and wv == 0:
                    e.wait(f"ycross:{cc - 1}")
                e.op("tensor_tensor",
                     bass.AP(t.msc, hs[0] * 128,
                             [[2048, 128], [256, 4], [1, 128]]),
                     ps[:, :].rearrange("q (a b) -> q a b", a=4),
                     t.mask[:, :].rearrange("q (a b) -> q a b", a=4),
                     OP.mult, mark=f"mev:{cc}:{wv}")
            e.drain_()
            e.op("tensor_tensor", t.ruk[:, :].rearrange("q (a b) -> q a b", a=G),
                 bass.AP(t.rhat, ci * 128, [[2048, 128], [256, G], [1, 128]]),
                 bass.AP(t.ktil, ci * 128, [[2048, 128], [256, G], [1, 128]]),
                 OP.mult, mark=f"rk:{cc}")
            e.wait(f"uu:{cc}")
            e.wait(f"ycross:{cc}")
            e.op("tensor_tensor", t.S[:, :], t.S[:, :], t.pu[:, :], OP.add)
            e.wait(f"elc:{cc}")
            e.drain_()
            for g in range(G):
                e.op("tensor_scalar", t.S[:, g * 64:(g + 1) * 64],
                     t.S[:, g * 64:(g + 1) * 64], t.elc[:, g:g + 1], None,
                     OP.mult, mark=(f"supd:{cc}" if g == G - 1 else None))
            e.drain_()
            e.op("tensor_copy", t.sbf[0:64, 0:512], t.S[0:64, :])
            e.op("tensor_copy", t.sbf[64:128, 512:1024], t.S[64:128, :],
                 mark=f"sbf:{cc}")
            e.wait(f"eg:{cc}")
            e.op("tensor_tensor", t.gpre[:, :], t.gpre[:, :],
                 bass.AP(t.lin, ci * 129 + 128, [[G * 258, 128], [258, G]]),
                 OP.add, mark=f"gpreu:{cc}")
            e.wait(f"c:{cc}")
            e.op("tensor_copy", t.cst[:, :], t.ppa[:, 0:16], mark=f"cout:{cc}")
            e.drain_()
            e.op("tensor_tensor", t.vc[:, :].rearrange("q (a b) -> q a b", a=H),
                 t.vtm[:, ci * 1024:(ci + 1) * 1024]
                 .rearrange("q (a b) -> q a b", a=H),
                 t.cst[:, :].unsqueeze(2).broadcast_to([128, H, 64]), OP.mult)
            if cc > 1:
                e.wait(f"spy:{cc - 2}")
            e.drain_()
            e.op("tensor_tensor", t.ysp[:, ci * 1024:(ci + 1) * 1024],
                 t.ppv[:, :], t.vc[:, :], OP.add, mark=f"yev:{cc}")
    e.s.mark("endA:vector", "VE")


def scalar_A(e, t, p):
    e.wait("ld:setup")
    e.wait("su:eps")
    for nm in ("k", "v", "r", "w"):
        e.op("activation", t["sg" + nm][:, :], t["mix" + nm + "8"][:, :],
             AF.Sigmoid, mark=f"su:sg{nm}")
    for it in range(2):
        e.wait(f"su:xpv{it}")
        e.op("activation", t.tiny[0:1, 4:5], t.tiny[0:1, 2:3], AF.Ln)
        e.op("activation", t.tiny[0:1, 4:5], t.tiny[0:1, 4:5], AF.Exp,
             scale=-0.5, mark=f"su:xps{it}")
    for ti in range(NT):
        ccs = (2 * ti, 2 * ti + 1)
        for ci, cc in enumerate(ccs):
            e.wait(f"x:{cc}")
            xv = t.xin[cc % 2][:, :]
            e.op("activation", t.dump[:, :], xv, AF.Copy,
                 accum_out=t.st1[:, 0:1])
            e.op("activation", t.dump[:, :], xv, AF.Square,
                 accum_out=t.st1[:, 1:2], mark=f"l1acc:{cc}")
            e.wait(f"l1var:{cc}")
            e.op("activation", t.st1[:, 3:4], t.st1[:, 5:6], AF.Ln, bias=t.epsc[:, 0:1])
            e.op("activation", t.st1[:, 3:4], t.st1[:, 3:4], AF.Exp,
                 scale=-0.5, mark=f"l1s:{cc}")
            e.wait(f"l1b:{cc}")
            e.op("activation", t.xh1[:, :], xv, AF.Identity,
                 scale=t.st1[:, 3:4], bias=t.st1[:, 4:5],
                 accum_out=t.st2[:, 0:1], mark=f"xdone:{cc}")
            e.op("activation", t.dump[:, :], t.xh1[:, :], AF.Square,
                 accum_out=t.st2[:, 1:2], mark=f"l2acc:{cc}")
            e.wait(f"l2var:{cc}")
            e.op("activation", t.st2[:, 3:4], t.st2[:, 5:6], AF.Ln, bias=t.epsc[:, 0:1])
            e.op("activation", t.st2[:, 3:4], t.st2[:, 3:4], AF.Exp,
                 scale=-0.5, mark=f"l2s:{cc}")
            e.wait(f"l2b:{cc}")
            if cc > 0:
                e.wait(f"xt:{cc - 1}")
            e.op("activation", t.xh2[:, :], t.xh1[:, :], AF.Identity,
                 scale=t.st2[:, 3:4], bias=t.st2[:, 4:5], mark=f"xh:{cc}")
            if DBG in (True, "a") and cc == 0:
                e.dma(p.dbg["st1"], t.st1[:, :])
                e.dma(p.dbg["st2"], t.st2[:, :])
                e.dma(p.dbg["xh1"], t.xh1[:, :])
                e.dma(p.dbg["xh2"], t.xh2[:, :])
                e.dma(p.dbg["xin"], t.xin[0][:, :])
                e.wait_all_dma()
        for ci, cc in enumerate(ccs):
            e.wait(f"w1:{ti}")
            if cc > 0:
                e.wait(f"scan:{cc - 1}")
            e.op("activation", t.expw[:, :],
                 bass.AP(t.w1, ci * 128, [[2048, 128], [256, G], [1, 128]]),
                 AF.Exp, mark=f"expw:{cc}")
        e.wait(f"scan:{2 * ti + 1}")
        if ti > 0:
            e.wait(f"kev:{ti - 1}:3")
            e.wait(f"rhat:{ti - 1}")
        e.op("activation", t.fr[:, :].rearrange("q (a b c) -> q a b c",
                                                a=G, b=2),
             bass.AP(t.lin, 0, [[G * 258, 128], [258, G], [129, 2], [1, 128]]),
             AF.Exp, mark=f"fr:{ti}")
        e.op("activation", t.fk[:, :].rearrange("q (a b c) -> q a b c",
                                                a=G, b=2),
             bass.AP(t.lin, 1, [[G * 258, 128], [258, G], [129, 2], [1, 128]]),
             AF.Exp, scale=-1.0, mark=f"fk:{ti}")
        for dp in range(4):
            ps = t.ppa if dp % 2 == 0 else t.ppb
            e.wait(f"rp:{ti}:{dp}")
            if dp > 0:
                e.wait(f"rhev:{ti}:{dp - 1}")
            for sub in range(2):
                dg = dp * 2 + sub
                e.op("activation", t.rsg[:, sub * 256:(sub + 1) * 256],
                     ps[:, sub * 256:(sub + 1) * 256], AF.Sigmoid,
                     bias=t.br8[:, dg:dg + 1],
                     mark=(f"rsig:{ti}:{dp}" if sub == 1 else None))
        for ci, cc in enumerate(ccs):
            e.wait(f"gev:{cc}")
            if cc > 1:
                e.wait(f"spg:{cc - 2}")
            e.op("activation", t.gsp[:, ci * 1024:(ci + 1) * 1024],
                 t.gst[:, :], AF.Sigmoid, mark=f"g:{cc}")
        for ci, cc in enumerate(ccs):
            if DO_B:
                e.wait(f"plg:{cc}")
                e.op("activation", t.dcy[:, :], t.dcy[:, :], AF.Exp,
                     mark=f"d:{cc}")
        for ci, cc in enumerate(ccs):
            if not DO_B:
                continue
            if cc > 0:
                e.wait(f"supd:{cc - 1}")
            e.op("activation", t.elc[:, :],
                 bass.AP(t.lin, ci * 129 + 128, [[G * 258, 128], [258, G]]),
                 AF.Exp, mark=f"elc:{cc}")
            e.wait(f"gpreu:{cc - 1}" if cc > 0 else "su:betab")
            e.op("activation", t.eg[:, cc * 8:(cc + 1) * 8], t.gpre[:, :],
                 AF.Exp, mark=f"eg:{cc}")
    e.s.mark("endA:scalar", "SE")


def gpsimd_A(e, t, p):
    for ti in range(NT):
        e.wait(f"tmix:{ti}")
        prev_reader = {"k": f"kp:{ti - 1}:3", "v": f"vp:{2 * ti - 1}",
                       "r": f"gp:{2 * ti - 1}", "w": f"wp:{ti - 1}:3"}
        for nm in ("k", "v", "r", "w"):
            e.wait(f"su:sg{nm}")
            if ti > 0:
                e.wait(prev_reader[nm])
                if nm == "r":
                    e.wait(f"rp:{ti - 1}:3")
            e.op("tensor_tensor",
                 t["mx" + nm][:, :].rearrange("q (a b) -> q a b", a=G),
                 t.tmx[:, :].rearrange("q (a b) -> q a b", a=G),
                 t["sg" + nm][:, :].unsqueeze(2).broadcast_to([128, G, 256]),
                 OP.mult, mark=f"gmix:{nm}:{ti}")
        for ci, cc in enumerate((2 * ti, 2 * ti + 1)):
            if not DO_B:
                continue
            e.wait(f"scan:{cc}")
            e.op("tensor_tensor", t.dcy[:, :].rearrange("q (a b) -> q a b", a=G),
                 bass.AP(t.lin, ci * 129 + 1, [[G * 258, 128], [258, G], [1, 128]]),
                 bass.AP(t.lin, ci * 129, [[G * 258, 128], [258, G], [1, 128]]),
                 OP.subtract, mark=f"plg:{cc}")
            e.wait(f"d:{cc}")
            e.op("tensor_tensor", t.dcy[:, :].rearrange("q (a b) -> q a b", a=G),
                 t.dcy[:, :].rearrange("q (a b) -> q a b", a=G),
                 t.u8[:, :].unsqueeze(2).broadcast_to([128, G, 128]),
                 OP.mult, mark=f"ud:{cc}")
            e.wait(f"rk:{cc}")
            if cc > 0:
                e.wait(f"c:{cc - 1}")
            e.op("tensor_tensor", t.ruk[:, :], t.ruk[:, :], t.dcy[:, :],
                 OP.mult, mark=f"ruk:{cc}")
    if DO_B and DO_C:
        e.wait("cc:S")
        e.op("collective_compute", "AllGather", OP.bypass,
             replica_groups=[list(range(8))], ins=[p.cc_in[:, :]],
             outs=[p.cc_out[:, :, :]], mark="cc:done")
    e.s.mark("endA:gpsimd", "GP")


def sync_C(e, t, p):
    for k in ("tensor", "vector", "scalar", "gpsimd"):
        e.wait(f"endA:{k}")
    e.dma(t.wo[:, :].rearrange("q (a d) -> q a d", a=G),
          p.dri["wo"].rearrange("(a q) d -> q a d", q=128), mark="ld:wo")
    for i in range(4):
        e.dma(t.wfc[:, i * 8192:(i + 1) * 8192]
              .rearrange("q (a d) -> q a d", a=2),
              p.dri["wfc"][i * 256:(i + 1) * 256, :]
              .rearrange("(a q) d -> q a d", q=128),
              mark=f"ld:wfc{i}")
    e.wait("cc:done")
    e.dma(t.sall[:, :].rearrange("q (r d) -> q r d", r=8),
          p.cc_out.rearrange("r q d -> q r d"), mark="ld:sall")
    for i in range(4):
        e.dma(t.wpj[:, i * 8192:(i + 1) * 8192]
              .rearrange("q (a d) -> q a d", a=8),
              p.dri["wpj"][i * 1024:(i + 1) * 1024, :]
              .rearrange("(a q) d -> q a d", q=128),
              mark=f"ld:wpj{i}")
    if DBG in (True, "c", "c2"):
        e.wait("ssel")
        e.dma(p.dbg["ssl"], t.ssl[:, :])
        e.dma(p.dbg["sallo"], t.sall[:, :])
        e.dma(p.dbg["eg"], t.eg[:, :])
    for cc in range(NCH):
        if DBG == "z1" and cc == 1:
            e.wait("rg:0")
        if DBG == "z1" and cc == 2:
            e.wait("x1:0")
            e.wait("ytv:0")
        if DBG == "z5" and cc == 1:
            e.wait("x1:0")
        if DBG in (True, "c", "c3") and cc == 1:
            e.wait("rg:0")
            e.dma(p.dbg["rg0"], t.rg[:, :])
            e.wait("dbg:rg0")
            e.dma(p.dbg["rgv"], t.dbgrg[:, :])
            e.wait("dbg:yt0")
            e.dma(p.dbg["ytv0"], t.dbgyt[:, :])
        if DBG in (True, "c", "c3") and cc == 2:
            e.wait("x1:0")
            e.dma(p.dbg["x10"], t.x1c[0][:, :])
            e.wait("ytv:0")
            e.dma(p.dbg["yt0"], t.yt[:, :])
        if cc >= 3:
            e.wait(f"out:{cc - 3}")
        e.dma(t.xc[cc % 2][:, :], p.dri["x"][cc * CH:(cc + 1) * CH, :])
        if cc >= 2:
            e.wait(f"ytv:{cc - 2}")
        e.dma(t.yre[cc % 2][:, :], p.y_sp[cc, :, :])
        if cc >= 2:
            e.wait(f"gy:{cc - 2}")
        e.dma(t.gre[cc % 2][:, :], p.g_sp[cc, :, :])
        if cc >= 2:
            e.wait(f"rg:{cc - 2}")
        e.dma(t.rre[cc % 2][:, :], p.r_sp[cc, :, :])
        e.fence()
        for nm in ("cx", "cy", "cg", "cr"):
            e.mark_dma(f"{nm}:{cc}")
        if cc >= 1:
            e.wait(f"outv:{cc - 1}")
            e.dma(p.dro[(cc - 1) * CH:cc * CH, :],
                  t.outst[(cc - 1) % 2][:, :])
            e.fence()
            e.mark_dma(f"out:{cc - 1}")
    e.wait(f"outv:{NCH - 1}")
    e.dma(p.dro[(NCH - 1) * CH:NCH * CH, :], t.outst[(NCH - 1) % 2][:, :],
          mark=f"out:{NCH - 1}")
    if DBG in (True, "a"):
        e.dma(p.dbg["ysp"], p.y_sp[:, :, :])
        e.dma(p.dbg["gsp"], p.g_sp[:, :, :])
        e.dma(p.dbg["rsp"], p.r_sp[:, :, :])
        e.dma(p.dbg["Send"], p.cc_in[:, :])
        e.wait(f"x1:{NCH - 1}")
        e.dma(p.dbg["x1"], t.x1c[1][:, :])
        e.wait(f"hsq:{NT - 1}:31")
        e.dma(p.dbg["hb"], t.hb[:, :])
    e.wait_all_dma()


def tensor_C(e, t, p):
    for k in ("vector", "scalar", "gpsimd"):
        e.wait(f"endA:{k}")
    g_ = {}

    def use(ps, freemark):
        e.wait(g_.get(ps))
        g_[ps] = freemark

    for cc in range(NCH):
        ci = cc % 2
        ti = cc // 2
        e.wait(f"rg:{cc}")
        e.wait("ssel")
        use("pdc", f"ytv:{cc}")
        for h in range(H):
            g = h // 2
            e.op("matmul", t.pdc[:, h * 64:(h + 1) * 64],
                 t.rg[:, g * 128:(g + 1) * 128],
                 t.ssl[:, (h % 2) * 512 + g * 64:(h % 2) * 512 + (g + 1) * 64],
                 start=True, stop=True,
                 mark=(f"corr:{cc}" if h == 15 else None))
        e.wait(f"gy:{cc}")
        use("pt", f"gytout:{cc}")
        for g in range(G):
            e.op("transpose", t.pt[:, g * 128:(g + 1) * 128],
                 t.gy[:, g * 128:(g + 1) * 128], t.ident[:, :],
                 mark=(f"gyt:{cc}" if g == G - 1 else None))
        e.wait(f"gytout:{cc}")
        e.wait("ld:wo")
        use("ph", f"x1:{cc}")
        for gi in range(G):
            for half in range(2):
                e.op("matmul", t.ph[:, half * 512:(half + 1) * 512],
                     t.gyt[:, gi * 128:(gi + 1) * 128],
                     t.wo[:, gi * 1024 + half * 512:gi * 1024 + half * 512 + 512],
                     start=(gi == 0), stop=(gi == 7),
                     mark=(f"wo:{cc}" if (gi == 7 and half == 1) else None))
        e.wait(f"xf:{cc}")
        use("pt", f"xftout:{cc}")
        for g in range(G):
            e.op("transpose", t.pt[:, g * 128:(g + 1) * 128],
                 t.xfb[:, g * 128:(g + 1) * 128], t.ident[:, :],
                 mark=(f"xft:{cc}" if g == G - 1 else None))
        if ci == 1:
            e.wait(f"xftout:{cc}")
            e.wait("ld:wfc3")
            for fg in range(FG):
                ps = "pha" if fg % 2 == 0 else "phb"
                use(ps, f"hrl:{ti}:{fg}")
                for gi in range(G):
                    e.op("matmul", t[ps][:, 0:256],
                         t.wfc[:, gi * 4096 + fg * 128:gi * 4096 + (fg + 1) * 128],
                         t.xft[:, gi * 256:(gi + 1) * 256],
                         start=(gi == 0), stop=(gi == 7),
                         mark=(f"hp:{ti}:{fg}" if gi == 7 else None))
            for c2 in (cc - 1, cc):
                ci2 = c2 % 2
                e.wait(f"hsq:{ti}:{FG - 1}")
                e.wait("ld:wpj3")
                for half in range(2):
                    use("py2", f"ysbh:{c2}:{half}")
                    for fg in range(FG):
                        e.op("matmul", t.py2[:, :],
                             t.hb[:, fg * 256 + ci2 * 128:fg * 256 + ci2 * 128 + 128],
                             t.wpj[:, fg * 1024 + half * 512:fg * 1024 + half * 512 + 512],
                             start=(fg == 0), stop=(fg == 31),
                             mark=(f"y2:{c2}:{half}" if fg == 31 else None))
    e.s.mark("endC:tensor", "TE")


def vector_C(e, t, p):
    for k in ("tensor", "scalar", "gpsimd"):
        e.wait(f"endA:{k}")
    e.wait("ld:sall")
    e.op("tensor_scalar", t.ssl[:, 0:512], t.sall[:, 0:512], t.sel[:, 0:1],
         None, OP.mult)
    for r in range(1, 8):
        e.drain_()
        e.op("scalar_tensor_tensor", t.ssl[:, 0:512],
             t.sall[:, r * 512:(r + 1) * 512], t.sel[:, r:r + 1],
             t.ssl[:, 0:512], OP.mult, OP.add)
    e.drain_()
    e.op("tensor_copy", t.ssl[64:128, 512:1024], t.ssl[64:128, 0:512])
    e.drain_()
    e.op("memset", t.ssl[64:128, 0:512], 0.0)
    e.op("memset", t.ssl[0:64, 512:1024], 0.0, mark="ssel")
    for cc in range(NCH):
        ci = cc % 2
        ti = cc // 2
        e.wait(f"cr:{cc}")
        e.wait(f"corr:{cc - 1}" if cc > 0 else "ssel")
        e.op("tensor_tensor", t.rg[:, :].rearrange("q (a b) -> q a b", a=G),
             t.rre[ci][:, :].rearrange("q (a b) -> q a b", a=G),
             t.eg[:, cc * 8:(cc + 1) * 8].unsqueeze(2)
             .broadcast_to([128, G, 128]), OP.mult, mark=f"rg:{cc}")
        if DBG in (True, "c", "c3") and cc == 0:
            e.drain_()
            e.op("tensor_copy", t.dbgrg[:, :], t.rg[:, :], mark="dbg:rg0")
        e.wait(f"corr:{cc}")
        e.wait(f"cy:{cc}")
        e.op("tensor_tensor", t.yt[:, :], t.pdc[:, :], t.yre[ci][:, :],
             OP.add, mark=f"ytv:{cc}")
        if DBG in (True, "c", "c3") and cc == 0:
            e.drain_()
            e.op("tensor_copy", t.dbgyt[:, :], t.yt[:, :], mark="dbg:yt0")
        e.wait(f"cg:{cc}")
        if cc > 0:
            e.wait(f"xft:{cc - 1}")
        e.drain_()
        e.op("tensor_tensor", t.gy[:, :], t.yt[:, :], t.gre[ci][:, :],
             OP.mult, mark=f"gy:{cc}")
        e.wait(f"gyt:{cc}")
        if cc > 0:
            e.wait(f"wo:{cc - 1}")
        e.op("tensor_copy", t.gyt[:, :],
             t.pt[:, :].rearrange("q (a b) -> q a b", a=G),
             mark=f"gytout:{cc}")
        e.wait(f"wo:{cc}")
        e.wait(f"cx:{cc}")
        if cc > 1:
            e.wait(f"outv:{cc - 2}")
        e.op("tensor_tensor", t.x1c[ci][:, :], t.ph[:, :], t.xc[ci][:, :],
             OP.add, mark=f"x1:{cc}")
        st = t.st3
        e.wait(f"l3acc:{cc}")
        e.op("scalar_tensor_tensor", st[:, 2:3], st[:, 0:1],
             1.0 / (1024.0 * 1024.0), st[:, 0:1], OP.mult, OP.mult)
        e.drain_()
        e.op("scalar_tensor_tensor", st[:, 5:6], st[:, 1:2], 1.0 / 1024.0,
             st[:, 2:3], OP.mult, OP.subtract, mark=f"l3var:{cc}")
        e.wait(f"l3s:{cc}")
        e.op("scalar_tensor_tensor", st[:, 4:5], st[:, 0:1], -1.0 / 1024.0,
             st[:, 3:4], OP.mult, OP.mult, mark=f"l3b:{cc}")
        e.wait(f"xft:{cc}")
        if ti > 0 and ci == 0:
            e.wait(f"hp:{ti - 1}:{FG - 1}")
        e.op("tensor_copy",
             bass.AP(t.xft, ci * 128, [[G * 256, 128], [256, G], [1, 128]]),
             t.pt[:, :].rearrange("q (a b) -> q a b", a=G),
             mark=f"xftout:{cc}")
        if ci == 1:
            for fg in range(FG):
                e.wait(f"hp:{ti}:{fg}")
                if fg > 1:
                    e.wait(f"hsq:{ti}:{fg - 2}")
                e.op("tensor_scalar", t.hrl[fg % 2][:, :],
                     (t.pha if fg % 2 == 0 else t.phb)[:, 0:256], 0.0, None,
                     OP.max, mark=f"hrl:{ti}:{fg}")
        if cc > 0:
            spike_V(e, t, cc - 1)
    spike_V(e, t, NCH - 1)
    e.s.mark("endC:vector", "VE")


def spike_V(e, t, cc):
    ci = cc % 2
    yv = [t.ysb[:, h * 512:(h + 1) * 512] for h in range(2)]
    m = [t.m[:, h * 512:(h + 1) * 512] for h in range(2)]
    sk = [t.sk[:, h * 512:(h + 1) * 512] for h in range(2)]
    ss = [t.ssum[:, h * 512:(h + 1) * 512] for h in range(2)]
    e.wait(f"ysbh:{cc}:0")
    e.wait(f"ysbh:{cc}:1")
    for h in range(2):
        e.op("tensor_single_scalar", sk[h], yv[h], 1.0, OP.is_gt)
    for h in range(2):
        e.op("scalar_tensor_tensor", m[h], yv[h], 1.9, sk[h], OP.mult,
             OP.subtract)
    for h in range(2):
        e.op("tensor_copy", ss[h], sk[h])
    for step in range(4):
        for h in range(2):
            e.op("tensor_single_scalar", sk[h], m[h], 1.0, OP.is_gt)
        for h in range(2):
            e.op("tensor_tensor", ss[h], ss[h], sk[h], OP.add)
        if step < 3:
            for h in range(2):
                e.op("tensor_tensor", m[h], m[h], sk[h], OP.subtract)
            for h in range(2):
                e.op("scalar_tensor_tensor", m[h], m[h], 0.9, yv[h], OP.mult,
                     OP.add)
    e.s.mark(f"usp:{cc}", "VE")
    if cc > 1:
        e.wait(f"out:{cc - 2}")
    e.drain_()
    e.op("scalar_tensor_tensor", t.outst[ci][:, :], t.ssum[:, :], 0.2,
         t.x1c[ci][:, :], OP.mult, OP.add, mark=f"outv:{cc}")


def scalar_C(e, t, p):
    for k in ("tensor", "vector", "gpsimd"):
        e.wait(f"endA:{k}")
    for ti in range(NT):
        for ci, cc in enumerate((2 * ti, 2 * ti + 1)):
            e.wait(f"x1:{cc}")
            e.op("activation", t.dump[:, :], t.x1c[ci][:, :], AF.Copy,
                 accum_out=t.st3[:, 0:1])
            e.op("activation", t.dump[:, :], t.x1c[ci][:, :], AF.Square,
                 accum_out=t.st3[:, 1:2], mark=f"l3acc:{cc}")
            e.wait(f"l3var:{cc}")
            e.op("activation", t.st3[:, 3:4], t.st3[:, 5:6], AF.Ln, bias=t.epsc[:, 0:1])
            e.op("activation", t.st3[:, 3:4], t.st3[:, 3:4], AF.Exp,
                 scale=-0.5, mark=f"l3s:{cc}")
            e.wait(f"l3b:{cc}")
            e.wait(f"gyt:{cc}")
            e.op("activation", t.xfb[:, :], t.x1c[ci][:, :], AF.Identity,
                 scale=t.st3[:, 3:4], bias=t.st3[:, 4:5], mark=f"xf:{cc}")
        for fg in range(FG):
            e.wait(f"hrl:{ti}:{fg}")
            if ti > 0 and fg == 0:
                e.wait(f"y2:{2 * ti - 1}:1")
            e.op("activation", t.hb[:, fg * 256:(fg + 1) * 256],
                 t.hrl[fg % 2][:, :], AF.Square, mark=f"hsq:{ti}:{fg}")
        for ci, cc in enumerate((2 * ti, 2 * ti + 1)):
            for half in range(2):
                e.wait(f"y2:{cc}:{half}")
                if cc > 0 and half == 0:
                    e.wait(f"outv:{cc - 1}")
                    e.wait(f"usp:{cc - 1}")
                e.op("activation", t.ysb[:, half * 512:(half + 1) * 512],
                     t.py2[:, :], AF.Copy, mark=f"ysbh:{cc}:{half}")
            e.s.mark(f"ysb:{cc}:0", "SE")
            e.s.mark(f"ysb:{cc}:1", "SE")
    e.s.mark("endC:scalar", "SE")


def gpsimd_C(e, t, p):
    for k in ("tensor", "vector", "scalar"):
        e.wait(f"endA:{k}")
    e.s.mark("endC:gpsimd", "GP")


def sim_deadlock(s, phase):
    seqs = {k: list(v) for k, v in s.trace.items()}
    pos = {k: 0 for k in seqs}
    cnt = dict(s.base_counts)
    progress = True
    while progress:
        progress = False
        for k in seqs:
            while pos[k] < len(seqs[k]):
                it = seqs[k][pos[k]]
                if it[0] == "inc":
                    cnt[it[1]] = cnt.get(it[1], 0) + it[2]
                else:
                    semkey, v = s.marks[it[1]]
                    if semkey != {"tensor": "TE", "vector": "VE",
                                  "scalar": "SE", "gpsimd": "GP",
                                  "sync": "SYQ"}.get(k, "?") and                             cnt.get(semkey, 0) < v:
                        break
                pos[k] += 1
                progress = True
    stuck = {k: (pos[k], len(seqs[k])) for k in seqs if pos[k] < len(seqs[k])}
    if stuck:
        det = {}
        for k in stuck:
            it = s.trace[k][pos[k]]
            det[k] = (it, s.marks.get(it[1]) if it[0] == "waitm" else None,
                      {sk: cnt.get(sk, 0) for sk in ("TE", "VE", "SE", "GP",
                                                     "SYQ", "SPQ", "GPQ")})
        raise RuntimeError(f"DEADLOCK in phase {phase}: {stuck}\n{det}")


def build_nc():
    nc = bass.Bass("TRN2", target_bir_lowering=False,
                   dynamic_dma_scratch_size=4096)
    p = type("P", (), {})()
    p.dri = {}
    for name, sh, dt in (
        ("x", [TH, D_], F32), ("xprev", [1, D_], F32),
        ("flag", [128, 1], F32), ("sel", [128, 8], F32),
        ("gamma8", [128, 8], F32), ("beta8", [128, 8], F32),
        ("mixk8", [128, 8], F32), ("mixv8", [128, 8], F32),
        ("mixr8", [128, 8], F32), ("mixw8", [128, 8], F32),
        ("td8", [128, 8], F32), ("u8", [128, 8], F32),
        ("mask", [128, 512], BF16), ("bpair", [128, 2], BF16),
        ("ident", [128, 128], BF16), ("ones_f", [128, 128], F32),
        ("ones_b", [128, 128], BF16),
        ("wk", [D_, D_], BF16), ("wv", [D_, D_], BF16),
        ("wr", [D_, D_], BF16), ("ww", [D_, D_], BF16),
        ("wg", [D_, D_], BF16), ("wo", [D_, D_], BF16),
        ("wfc", [D_, FF], BF16), ("wpj", [FF, D_], BF16),
    ):
        p.dri[name] = nc.dram_tensor(name, sh, dt, kind="ExternalInput").ap()
    p.dro = nc.dram_tensor("out", [TH, D_], F32, kind="ExternalOutput").ap()
    p.y_sp = nc.dram_tensor("y_sp", [NCH, 128, D_], BF16, kind="Internal").ap()
    p.g_sp = nc.dram_tensor("g_sp", [NCH, 128, D_], BF16, kind="Internal").ap()
    p.r_sp = nc.dram_tensor("r_sp", [NCH, 128, D_], BF16, kind="Internal").ap()
    p.cc_in = nc.dram_tensor("cc_in", [128, 512], F32, kind="Internal").ap()
    p.cc_out = nc.dram_tensor("cc_out", [8, 128, 512], F32, kind="Internal",
                              addr_space="Shared").ap()
    p.dbg = {}
    if DBG:
        for nm, sh, dt in (
            ("ktil", [128, 2048], BF16), ("rhat", [128, 2048], BF16),
            ("vtm", [128, 2048], BF16), ("lin", [128, G * 258], F32),
            ("mxk", [128, 2048], BF16), ("mxw", [128, 2048], BF16),
            ("w1", [128, 2048], F32), ("fk", [128, 2048], BF16),
            ("fr", [128, 2048], BF16), ("msc", [128, 2048], BF16),
            ("xsb", [128, G * 257], BF16), ("cst", [128, 16], F32),
            ("ktm", [128, 1024], BF16), ("S2", [128, 512], F32),
            ("ysp", [NCH, 128, D_], BF16), ("gsp", [NCH, 128, D_], BF16),
            ("rsp", [NCH, 128, D_], BF16), ("Send", [128, 512], F32),
            ("x1", [128, 1024], F32), ("hb", [128, FG * 256], BF16),
            ("st1", [128, 8], F32), ("st2", [128, 8], F32),
            ("xh1", [128, 1024], F32), ("xh2", [128, 1024], BF16),
            ("xin", [128, 1024], F32),
            ("ssl", [128, 1024], BF16), ("sallo", [128, 4096], F32),
            ("eg", [128, 8 * NCH], F32), ("rgv", [128, 1024], BF16),
            ("ytv0", [128, 1024], BF16),
            ("rg0", [128, 1024], BF16), ("x10", [128, 1024], F32),
            ("yt0", [128, 1024], BF16),
        ):
            p.dbg[nm] = nc.dram_tensor("dbg_" + nm, sh, dt,
                                       kind="ExternalOutput").ap()

    s = Sched()
    with ExitStack() as PS:
        t = Tens(nc)
        for k in ("TE", "VE", "SE", "GP", "SYQ", "SPQ", "GPQ"):
            s.sems[k] = PS.enter_context(nc.semaphore(k))
        for name, sh, dt in (
            ("eg", [128, 8 * NCH], F32),
            ("ident", [128, 128], BF16),
            ("sel", [128, 8], F32), ("epsc", [128, 1], F32),
            ("st3", [128, 8], F32), ("dump", [128, 1024], F32),
        ):
            t.add(PS, name, sh, dt)

        with ExitStack() as SA:
            for name, sh, dt in (
                ("S", [128, 512], F32), ("sbf", [128, 1024], BF16),
                ("gpre", [128, 8], F32), ("elc", [128, 8], F32),
                ("ones_f", [128, 128], F32), ("ones_b", [128, 128], BF16),
                ("bpair", [128, 2], BF16), ("mask", [128, 512], BF16),
                ("flag", [128, 1], F32),
                ("gamma8", [128, 8], F32), ("beta8", [128, 8], F32),
                ("mixk8", [128, 8], F32), ("mixv8", [128, 8], F32),
                ("mixr8", [128, 8], F32), ("mixw8", [128, 8], F32),
                ("td8", [128, 8], F32), ("u8", [128, 8], F32),
                ("prevcol", [128, 8], F32), ("tiny", [1, 8], F32),
                ("st1", [128, 8], F32), ("st2", [128, 8], F32),
                ("sgk", [128, 8], F32), ("sgv", [128, 8], F32),
                ("sgr", [128, 8], F32), ("sgw", [128, 8], F32),
                ("bk8", [128, 8], F32), ("br8", [128, 8], F32),
                ("bw8", [128, 8], F32), ("tdb8", [128, 8], F32),
                ("beta_b", [128, 8], BF16),
                ("wk", [128, G * 1024], BF16), ("wv", [128, G * 1024], BF16),
                ("wr", [128, G * 1024], BF16), ("ww", [128, G * 1024], BF16),
                ("wg", [128, G * 1024], BF16),
                ("xin0", [128, 1024], F32), ("xin1", [128, 1024], F32),
                ("xh1", [128, 1024], F32), ("xh2", [128, 1024], BF16),
                ("xsb", [128, G * 257], BF16),
                ("tmx", [128, 2048], BF16),
                ("mxk", [128, 2048], BF16), ("mxv", [128, 2048], BF16),
                ("mxr", [128, 2048], BF16), ("mxw", [128, 2048], BF16),
                ("w1", [128, 2048], F32), ("expw", [128, 1024], F32),
                ("lin", [128, G * 258], F32),
                ("fr", [128, 2048], BF16), ("fk", [128, 2048], BF16),
                ("rsg", [128, 512], BF16),
                ("ktil", [128, 2048], BF16), ("ktil_e", [128, 2048], BF16),
                ("ktil_o", [128, 2048], BF16), ("ktm", [128, 1024], BF16),
                ("rhat", [128, 2048], BF16),
                ("vtm", [128, 2048], BF16),
                ("dcy", [128, 1024], BF16), ("ruk", [128, 1024], BF16),
                ("msc", [128, 2048], BF16),
                ("cst", [128, 16], F32), ("vc", [128, 1024], BF16),
                ("gst", [128, 1024], BF16),
                ("gsp", [128, 2048], BF16), ("ysp", [128, 2048], BF16),
                ("xprevpad", [128, 1024], BF16),
                ("bvfull", [128, 1024], BF16), ("bgfull", [128, 1024], BF16),
            ):
                t.add(SA, name, sh, dt)
            t.d["xin"] = [t.d["xin0"], t.d["xin1"]]
            t.alias("tinyb", "dump")
            t.alias("xprevrow", "xh1")
            t.d["bvrow"] = t.d["fr"][0:1, :]
            t.d["bgrow"] = t.d["fr"][0:1, :]
            t.addp(SA, "pst", [128, 1024], BF16)
            t.addp(SA, "ppa", [128, 512], F32)
            t.addp(SA, "ppb", [128, 512], F32)
            t.addp(SA, "ppv", [128, 1024], F32)
            t.addp(SA, "pca", [128, 512], F32)
            t.addp(SA, "pcb", [128, 512], F32)
            t.addp(SA, "pu", [128, 512], F32)

            emitters_A = (("sync", sync_A), ("tensor", tensor_A),
                          ("vector", vector_A), ("scalar", scalar_A),
                          ("gpsimd", gpsimd_A))
            s.start(True)
            s.base_counts = {}
            for key, fn in emitters_A:
                fn(Eng(s, key, None), t, p)
            countsA = dict(s.counts)
            sim_deadlock(s, "A")
            traceA = {k: list(v) for k, v in s.trace.items()}
            s.start(False)
            real = {"sync": nc.sync, "tensor": nc.tensor, "vector": nc.vector,
                    "scalar": nc.scalar, "gpsimd": nc.gpsimd}
            with nc.Block() as blk:
                for key, fn in emitters_A:
                    getattr(blk, key)(lambda eng, fn=fn, key=key:
                                      fn(Eng(s, key, eng), t, p))
            countsA2 = dict(s.counts)
            assert countsA == countsA2, (countsA, countsA2)

        if not DO_C:
            return nc
        with ExitStack() as SC:
            for name, sh, dt in (
                ("wfc", [128, FG * 1024], BF16), ("wpj", [128, FG * 1024], BF16),
                ("wo", [128, G * 1024], BF16), ("ssl", [128, 1024], BF16),
                ("xc0", [128, 1024], F32), ("xc1", [128, 1024], F32),
                ("yre0", [128, 1024], BF16), ("yre1", [128, 1024], BF16),
                ("gre0", [128, 1024], BF16), ("gre1", [128, 1024], BF16),
                ("rre0", [128, 1024], BF16), ("rre1", [128, 1024], BF16),
                ("rg", [128, 1024], BF16),
                ("gy", [128, 1024], BF16), ("gyt", [128, 1024], BF16),
                ("x10", [128, 1024], F32), ("x11", [128, 1024], F32),
                ("xft", [128, 2048], BF16),
                ("hb", [128, FG * 256], BF16),
                ("hrl0", [128, 256], BF16), ("hrl1", [128, 256], BF16),
                ("ysb", [128, 1024], F32), ("m", [128, 1024], F32),
                ("sk", [128, 1024], BF16), ("ssum", [128, 1024], BF16),
            ):
                t.add(SC, name, sh, dt)
            t.d["sall"] = t.d["hb"][:, :].bitcast(F32)
            t.d["x1c"] = [t.d["x10"], t.d["x11"]]
            t.d["dbgrg"] = t.d["m"][:, 0:512].bitcast(BF16)
            t.d["dbgyt"] = t.d["m"][:, 512:1024].bitcast(BF16)
            t.d["xc"] = [t.d["xc0"], t.d["xc1"]]
            t.d["yre"] = [t.d["yre0"], t.d["yre1"]]
            t.d["gre"] = [t.d["gre0"], t.d["gre1"]]
            t.d["rre"] = [t.d["rre0"], t.d["rre1"]]
            t.d["hrl"] = [t.d["hrl0"], t.d["hrl1"]]
            t.d["outst"] = [t.d["xc1"], t.d["xc0"]]
            t.alias("yt", "rg")
            t.alias("xfb", "gy")
            t.addp(SC, "pt", [128, 1024], BF16)
            t.addp(SC, "pdc", [128, 1024], F32)
            t.addp(SC, "ph", [128, 1024], F32)
            t.addp(SC, "pha", [128, 256], F32)
            t.addp(SC, "phb", [128, 256], F32)
            t.addp(SC, "py2", [128, 512], F32)

            emitters_C = (("sync", sync_C), ("tensor", tensor_C),
                          ("vector", vector_C), ("scalar", scalar_C),
                          ("gpsimd", gpsimd_C))
            saved = dict(s.counts)
            s.rec = True
            s.trace = {k: list(v) for k, v in traceA.items()}
            for key, fn in emitters_C:
                fn(Eng(s, key, None), t, p)
            countsC = dict(s.counts)
            s.base_counts = {}
            sim_deadlock(s, "C")
            s.rec = False
            s.counts = saved
            with nc.Block() as blk:
                for key, fn in emitters_C:
                    getattr(blk, key)(lambda eng, fn=fn, key=key:
                                      fn(Eng(s, key, eng), t, p))
            assert s.counts == countsC, (s.counts, countsC)
    return nc


_NC_CACHE = []


def kernel(**inputs):
    xs = np.ascontiguousarray(np.asarray(inputs["L0_attn_x"], np.float32))
    gamma = np.asarray(inputs["L0_attn_ln_w"], np.float32).reshape(D_)
    beta = np.asarray(inputs["L0_attn_ln_b"], np.float32).reshape(D_)
    mixk = np.asarray(inputs["L0_attn_mix_k"], np.float32).reshape(D_)
    mixv = np.asarray(inputs["L0_attn_mix_v"], np.float32).reshape(D_)
    mixr = np.asarray(inputs["L0_attn_mix_r"], np.float32).reshape(D_)
    mixw = np.asarray(inputs["L0_attn_mix_w"], np.float32).reshape(D_)
    td = np.asarray(inputs["L0_attn_time_decay"], np.float32).reshape(D_)
    tf = np.asarray(inputs["L0_attn_time_first"], np.float32).reshape(D_)

    def col8(v):
        return np.ascontiguousarray(v.reshape(8, 128).T)

    wT = {k: np.ascontiguousarray(
        np.asarray(inputs[f"L0_attn_W{k}"], np.float32).T).astype(BF)
        for k in ("r", "k", "v", "w", "g", "o")}
    wfcT = np.ascontiguousarray(
        np.asarray(inputs["L0_ffn_W_fc"], np.float32).T).astype(BF)
    wpjT = np.ascontiguousarray(
        np.asarray(inputs["L0_ffn_W_proj"], np.float32).T).astype(BF)

    mask = np.zeros((128, 512), np.float32)
    for blkk in range(4):
        for ss in range(128):
            mask[ss, blkk * 128 + ss + 1:(blkk + 1) * 128] = 1.0
    bpair = np.zeros((128, 2), np.float32)
    bpair[0:64, 0] = 1.0
    bpair[64:128, 1] = 1.0
    common = {
        "gamma8": col8(gamma), "beta8": col8(beta),
        "mixk8": col8(mixk), "mixv8": col8(mixv),
        "mixr8": col8(mixr), "mixw8": col8(mixw),
        "td8": col8(td), "u8": col8(tf),
        "mask": mask.astype(BF), "bpair": bpair.astype(BF),
        "ident": np.eye(128, dtype=np.float32).astype(BF),
        "ones_f": np.ones((128, 128), np.float32),
        "ones_b": np.ones((128, 128), np.float32).astype(BF),
        "wk": wT["k"], "wv": wT["v"], "wr": wT["r"], "ww": wT["w"],
        "wg": wT["g"], "wo": wT["o"], "wfc": wfcT, "wpj": wpjT,
    }
    in_maps = []
    for c in range(8):
        b, half = c // 2, c % 2
        m = dict(common)
        m["x"] = np.ascontiguousarray(xs[b, half * TH:(half + 1) * TH, :])
        m["xprev"] = (np.ascontiguousarray(xs[b, half * TH - 1:half * TH, :])
                      if half else np.zeros((1, D_), np.float32))
        m["flag"] = np.full((128, 1), float(half), np.float32)
        sel = np.zeros((128, 8), np.float32)
        if half == 1:
            sel[:, c - 1] = 1.0
        m["sel"] = sel
        in_maps.append(m)

    if not _NC_CACHE:
        _NC_CACHE.append(build_nc())
    nc = _NC_CACHE[0]
    res = run_bass_kernel_spmd(nc, in_maps, core_ids=list(range(8)))
    out = np.empty((B_, T_, D_), np.float32)
    for c in range(8):
        b, half = c // 2, c % 2
        out[b, half * TH:(half + 1) * TH, :] = res.results[c]["out"]
    return out
